# version 05 (final)
# speedup vs baseline: 1.0257x; 1.0189x over previous
import math
import numpy as np
from contextlib import ExitStack

import concourse.bass as bass
import concourse.mybir as mybir
from concourse.bass_utils import run_bass_kernel_spmd

F32 = mybir.dt.float32
BF16 = mybir.dt.bfloat16
I32 = mybir.dt.int32
F16 = mybir.dt.float16
AF = mybir.ActivationFunctionType
ALU = mybir.AluOpType
AX = mybir.AxisListType

N_CORES = 8
T = 2048
D = 1024
DCH = D // 128



ENGINES = ("tensor", "vector", "scalar", "gpsimd", "sync")


class Buf:

    __slots__ = ("name", "tensor", "lo", "hi", "last_writer", "readers")

    def __init__(self, name, tensor=None, lo=0, hi=1 << 40):
        self.name = name
        self.tensor = tensor if tensor is not None else name
        self.lo = lo
        self.hi = hi
        self.last_writer = None
        self.readers = []

    def overlaps(self, other):
        return self.lo < other.hi and other.lo < self.hi


class Prog:
    def __init__(self, nc, ctx):
        self.nc = nc
        self.ctx = ctx
        self.streams = {e: [] for e in ENGINES}
        self.counts = {e: 0 for e in ENGINES}
        self.sems = {}
        for e in ENGINES:
            self.sems[e] = ctx.enter_context(nc.semaphore(f"sem_{e}"))
        self.NSEM = 20
        self.dma_sems = {}
        self.dma_counts = {}
        self.dma_idx = {}
        for q in ("sync", "gpsimd", "scalar", "vector"):
            self.dma_sems[q] = [ctx.enter_context(nc.semaphore(f"dsem_{q}{j}"))
                                for j in range(self.NSEM)]
            for j in range(self.NSEM):
                self.dma_counts[(q, j)] = 0
            self.dma_idx[q] = 0
        self.waited = {e: {} for e in ENGINES}
        self.bufs = {}
        self.registry = {}
        self._sb = {}
        self._ps = {}
        self.n_ops = 0

    def sb(self, name, shape, dtype):
        if name in self._sb:
            t, sh, dt = self._sb[name]
            assert list(sh) == list(shape) and dt == dtype, (name, sh, shape)
            return t
        t = self.ctx.enter_context(self.nc.sbuf_tensor(name, shape, dtype))
        self._sb[name] = (t, shape, dtype)
        self.buf(name)
        return t

    def sb_at(self, name, shape, dtype, offset):
        if name in self._sb:
            t, sh, dt = self._sb[name]
            assert list(sh) == list(shape) and dt == dtype, (name, sh, shape)
            return t
        t = self.nc.alloc_sbuf_tensor_at(name, shape, dtype, offset=offset)
        self._sb[name] = (t, shape, dtype)
        self.buf(name)
        return t

    def sb_addr(self, t):
        return self.nc.lookup_mloc(t).addr

    def layer_barrier(self):
        for e in ENGINES:
            self.wait_all(e)

    def ps(self, name, shape, dtype=F32):
        if name in self._ps:
            t, sh, dt = self._ps[name]
            assert list(sh) == list(shape) and dt == dtype
            return t
        t = self.ctx.enter_context(self.nc.psum_tensor(name, shape, dtype))
        self._ps[name] = (t, shape, dtype)
        self.buf(name)
        return t

    def buf(self, name):
        if isinstance(name, tuple):
            return self.region(*name)
        if name not in self.bufs:
            b = Buf(name)
            self.bufs[name] = b
            self.registry.setdefault(b.tensor, []).append(b)
        return self.bufs[name]

    def region(self, tensor, lo, hi):
        key = f"{tensor}[{lo}:{hi}]"
        if key not in self.bufs:
            b = Buf(key, tensor, lo, hi)
            self.bufs[key] = b
            self.registry.setdefault(tensor, []).append(b)
        return self.bufs[key]

    def _overlapping(self, b):
        return [b2 for b2 in self.registry.get(b.tensor, ()) if b2.overlaps(b)]

    def _dep_tokens(self, reads, writes):
        toks = []
        for r in reads:
            b = self.buf(r)
            for b2 in self._overlapping(b):
                if b2.last_writer is not None:
                    toks.append(b2.last_writer)
        for w in writes:
            b = self.buf(w)
            for b2 in self._overlapping(b):
                if b2.last_writer is not None:
                    toks.append(b2.last_writer)
                toks.extend(b2.readers)
        return toks

    def _record(self, token, reads, writes):
        for r in reads:
            b = self.buf(r)
            b.readers.append(token)
        for w in writes:
            b = self.buf(w)
            b.last_writer = token
            b.readers = []

    def _waits_for(self, engine, toks):
        need = {}
        for kind, src, val in toks:
            if kind == "dma":
                val = self.dma_counts[src]
            key = (kind, src)
            if val > need.get(key, -1):
                need[key] = val
        waits = []
        dma_keys = []
        wm = self.waited[engine]
        for (kind, src), val in sorted(need.items(), key=str):
            if kind == "dma":
                dma_keys.append(src)
            if wm.get((kind, src), -1) >= val:
                continue
            wm[(kind, src)] = val
            sem = self.sems[src] if kind == "eng" else self.dma_sems[src[0]][src[1]]
            waits.append((sem, val))
        return waits, dma_keys

    def op(self, engine, fn, reads=(), writes=(), tag=None):
        toks = self._dep_tokens(reads, writes)
        waits, dma_keys = self._waits_for(engine, toks)
        self.counts[engine] += 1
        myval = self.counts[engine]
        token = ("eng", engine, myval)
        self._record(token, reads, writes)
        for dk in dma_keys:
            self.buf(f"#sem/{dk[0]}/{dk[1]}").readers.append(token)
        self.streams[engine].append(("op", waits, fn, self.sems[engine], myval, tag))
        self.n_ops += 1
        return token

    def dma(self, queue, out_ap, in_ap, reads=(), writes=(), **kw):
        j = self.dma_idx[queue] % self.NSEM
        self.dma_idx[queue] += 1
        semkey = (queue, j)
        semres = f"#sem/{queue}/{j}"
        toks = self._dep_tokens(list(reads), list(writes) + [semres])
        waits, dma_keys = self._waits_for(queue, toks)
        self.dma_counts[semkey] += 16
        myval = self.dma_counts[semkey]
        token = ("dma", semkey, myval)
        self._record(token, reads, list(writes) + [semres])
        for dk in dma_keys:
            self.buf(f"#sem/{dk[0]}/{dk[1]}").readers.append(token)
        self.streams[queue].append(
            ("dma", waits, out_ap, in_ap, kw, self.dma_sems[queue][j], myval))
        self.n_ops += 1
        return token

    def barrier_token(self, engine):
        return ("eng", engine, self.counts[engine])

    def wait_all(self, engine, engines=None):
        toks = []
        for e in engines or ENGINES:
            if e != engine and self.counts[e] > 0:
                toks.append(("eng", e, self.counts[e]))
        for k, v in self.dma_counts.items():
            if v > 0:
                toks.append(("dma", k, v))
        waits, _ = self._waits_for(engine, toks)
        if waits:
            self.streams[engine].append(("waits", waits))

    def finalize(self, final_engine="sync"):
        self.wait_all(final_engine)
        block = self.ctx.enter_context(self.nc.Block())
        for engine in ENGINES:
            stream = self.streams[engine]
            if not stream:
                continue

            def make_body(stream):
                def body(eng):
                    for item in stream:
                        if item[0] == "op":
                            _, waits, fn, sem, val, tag = item
                            for s, v in waits:
                                eng.wait_ge(s, v)
                            inst = fn(eng)
                            assert inst is not None, f"op fn returned None ({tag})"
                            inst.then_inc(sem, 1)
                        elif item[0] == "dma":
                            _, waits, out_ap, in_ap, kw, sem, val = item
                            for s, v in waits:
                                eng.wait_ge(s, v)
                            eng.dma_start(out=out_ap, in_=in_ap, **kw).then_inc(sem, 16)
                        else:
                            _, waits = item
                            for s, v in waits:
                                eng.wait_ge(s, v)
                return body
            getattr(block, engine)(make_body(stream))



def cdiv(a, b):
    return (a + b - 1) // b


def bAP(ap, newdims):
    return bass.AP(ap.tensor, ap.offset, newdims)


class Kb:

    def __init__(self, P, dram):
        self.P = P
        self.d = dram
        self.nc = P.nc
        p = P
        self.psA = p.ps("psA", [128, 2048], F32)
        self.pb0 = p.ps("pb0", [128, 512], F32)
        self.pb1 = p.ps("pb1", [128, 512], F32)
        self.pbX = p.ps("pbX", [128, 512], F32)
        self.pbT = p.ps("pbT", [128, 1024], BF16)
        self.ident = p.sb("c_ident", [128, 128], BF16)
        self.identf = p.sb("c_identf", [128, 128], F32)
        self.ones_row = p.sb("c_ones", [1, 128], F32)
        self.ones_rowb = p.sb("c_onesb", [1, 128], BF16)
        self.ones_colb = p.sb("c_onescol", [128, 1], BF16)
        self.iota_row = None
        self.cmask = p.sb("c_cmask", [128, 128], BF16)
        self.cmaskT = p.sb("c_cmaskT", [128, 128], BF16)
        self.arb1 = p.sb("arb1", [128, 16384], BF16)
        self.arf2 = p.sb("arf2", [128, 4096], F32)
        self.warena = p.sb("warena", [128, 3072], BF16)
        self.atS = p.sb("at_S", [128, 2048], F32)
        self.atP = p.sb("at_P", [128, 2048], BF16)
        self.atPT = p.sb("at_PT", [128, 2048], BF16)
        self.atC = p.sb("at_cols", [128, 8], F32)
        self.lsc = p.sb("lscratch", [128, 6032], F32)
        self.lsc_base = p.sb_addr(self.lsc)
        self._wbuf = {}
        self._nt_ctr = 0

    def sc(self, name, shape, dtype, off):
        bpp = {F32: 4, I32: 4, BF16: 2, F16: 2}[dtype]
        assert off % 32 == 0 and off + shape[1] * bpp <= 6032 * 4, (name, off)
        return self.P.sb_at(name, shape, dtype, self.lsc_base + off)

    def ensure_iota(self):
        if self.iota_row is None:
            self.iota_row = self.sc("c_iota", [128, 640], F16, 22432)
            self.P.dma("gpsimd", self.iota_row.ap(), self.d["C_iota"][:, :],
                       writes=["c_iota"])
        return self.iota_row

    def load_consts(self):
        P = self.P
        for nm, sb in [("C_ident", self.ident), ("C_identf", self.identf),
                       ("C_ones", self.ones_row), ("C_onesb", self.ones_rowb),
                       ("C_cmask", self.cmask), ("C_cmaskT", self.cmaskT)]:
            P.dma("sync", sb.ap(), self.d[nm][:, :], writes=[sb.name])

    @staticmethod
    def host_consts():
        ii = np.eye(128, dtype=np.float32)
        cm = np.where(np.arange(128)[None, :] <= np.arange(128)[:, None], 0.0, -1e30)
        return {
            "C_ident": ii,
            "C_identf": ii,
            "C_ones": np.ones((1, 128), np.float32),
            "C_onesb": np.ones((1, 128), np.float32),
            "C_iota": np.tile(np.arange(640, dtype=np.float32)[None, :], (128, 1)),
            "C_cmask": cm.astype(np.float32),
            "C_cmaskT": cm.T.astype(np.float32).copy(),
            "C_onescol": np.ones((128, 1), np.float32),
        }

    def wslot(self, i):
        return (self.warena.ap()[:, i * 1024:(i + 1) * 1024],
                ("warena", i * 1024, (i + 1) * 1024))

    def row_const(self, dram_name, n):
        ap = self.arf2.ap()[0:1, 0:n]
        self.P.dma("sync", ap, self.d[dram_name][:, 0:n], writes=[("arf2", 0, 2048)])
        return ap, ("arf2", 0, 2048)

    def stream_w32(self, dram2d, k0, K, m0, Mw, dst_ap, dst_name):
        return self.stream_w(dram2d, k0, K, m0, Mw, dst_ap, dst_name, queue="sync")

    def stream_w(self, dram2d, k0, K, m0, Mw, dst_ap, dst_name, queue="gpsimd"):
        if queue == "gpsimd" and dram2d.tensor.dtype == BF16:
            queue = "sync"
        if K <= 128:
            return self.P.dma(queue, dst_ap[0:K, 0:Mw], dram2d[k0:k0 + K, m0:m0 + Mw],
                              writes=[dst_name])
        nch = K // 128
        rem = K - nch * 128
        src = dram2d[k0:k0 + nch * 128, m0:m0 + Mw].rearrange("(c p) m -> p c m", p=128)
        dstv = dst_ap[:, 0:nch * Mw].rearrange("p (c m) -> p c m", c=nch)
        tok = self.P.dma(queue, dstv, src, writes=[dst_name])
        if rem:
            tok = self.P.dma(queue, dst_ap[0:rem, nch * Mw:(nch + 1) * Mw],
                             dram2d[k0 + nch * 128:k0 + K, m0:m0 + Mw],
                             writes=[dst_name])
        return tok

    def bcast_row(self, row_ap, W, out_ap, out_buf, pn=128, reads=(), extra_writes=(),
                  ev_engine="vector"):
        P = self.P
        nt = cdiv(W, 512)
        for i in range(nt):
            n0, n1 = i * 512, min((i + 1) * 512, W)
            P.op("tensor", (lambda n0=n0, n1=n1: lambda t: t.matmul(
                self.pbX.ap()[0:pn, 0:n1 - n0], self.ones_row.ap()[:, 0:pn],
                row_ap[:, n0:n1], start=True, stop=True))(),
                reads=list(reads) + ["c_ones"], writes=["pbX"])
            P.op(ev_engine, (lambda n0=n0, n1=n1: lambda v: v.tensor_copy(
                out_ap[:, n0:n1], self.pbX.ap()[0:pn, 0:n1 - n0]))(),
                reads=["pbX"], writes=list(extra_writes) + [out_buf])

    def transpose_b(self, in_ap, slot, pi=128, fi=128, reads=()):
        P = self.P
        return P.op("tensor", lambda t: t.transpose(
            self.pbT.ap()[0:fi, slot * 128:slot * 128 + pi], in_ap, self.ident.ap()),
            reads=list(reads) + ["c_ident"], writes=["pbT"])



def emit_ln(kb, xres, xnbf, nm_bc, rs_bc, tag):
    P = kb.P
    Tl = T
    for c in range(DCH):
        P.op("vector", (lambda c=c: lambda v: v.tensor_copy(
            xnbf.ap()[:, c * Tl:(c + 1) * Tl], xres.ap()[:, c * Tl:(c + 1) * Tl]))(),
            reads=[xres.name], writes=[xnbf.name], tag=f"{tag}_cp{c}")
    sqbf = kb.atP
    sA = kb.arf2.ap()[0:1, 0:Tl]
    sB = kb.arf2.ap()[0:1, 0:Tl]
    sTmp = kb.atS.ap()[0:1, 0:Tl]
    for c in range(DCH):
        for i in range(4):
            n0 = i * 512
            P.op("tensor", (lambda c=c, n0=n0: lambda t: t.matmul(
                kb.psA.ap()[0:1, n0:n0 + 512],
                kb.ones_colb.ap(),
                xnbf.ap()[:, c * Tl + n0:c * Tl + n0 + 512],
                start=(c == 0), stop=(c == DCH - 1)))(),
                reads=[xnbf.name, "c_onescol"], writes=["psA"], tag=f"{tag}_sum")
    P.op("vector", lambda v: v.tensor_scalar_mul(sA, kb.psA.ap()[0:1, 0:Tl], -1.0 / D),
         reads=["psA"], writes=[("arf2", 0, 2048)], tag=f"{tag}_nm")
    P.op("vector", lambda v: v.tensor_mul(sTmp, sA, sA),
         reads=[("arf2", 0, 2048)], writes=["at_S"])
    kb.bcast_row(sA, Tl, nm_bc.ap(), nm_bc.name, reads=[("arf2", 0, 2048)])
    for c in range(DCH):
        P.op("scalar", (lambda c=c: lambda s: s.activation(
            sqbf.ap(), xres.ap()[:, c * Tl:(c + 1) * Tl], AF.Square))(),
            reads=[xres.name], writes=["at_P"], tag=f"{tag}_sq")
        for i in range(4):
            n0 = i * 512
            P.op("tensor", (lambda c=c, n0=n0: lambda t: t.matmul(
                kb.psA.ap()[0:1, n0:n0 + 512],
                kb.ones_colb.ap(),
                sqbf.ap()[:, n0:n0 + 512],
                start=(c == 0), stop=(c == DCH - 1)))(),
                reads=["at_P", "c_onescol"], writes=["psA"], tag=f"{tag}_sumsq")
    P.op("vector", lambda v: v.scalar_tensor_tensor(
        sB, kb.psA.ap()[0:1, 0:Tl], 1.0 / D, sTmp, ALU.mult, ALU.subtract),
        reads=["psA", "at_S"], writes=[("arf2", 0, 2048)], tag=f"{tag}_var")
    P.op("vector", lambda v: v.tensor_scalar_add(sB, sB, 1e-5),
         reads=[("arf2", 0, 2048)], writes=[("arf2", 0, 2048)])
    P.op("scalar", lambda s: s.activation(sB, sB, AF.Sqrt),
         reads=[("arf2", 0, 2048)], writes=[("arf2", 0, 2048)])
    P.op("vector", lambda v: v.reciprocal(sB, sB),
         reads=[("arf2", 0, 2048)], writes=[("arf2", 0, 2048)])
    kb.bcast_row(sB, Tl, rs_bc.ap(), rs_bc.name, reads=[("arf2", 0, 2048)])
    xsc = kb.atS
    for c in range(DCH):
        P.op("vector", (lambda c=c: lambda v: v.tensor_add(
            xsc.ap()[:, 0:Tl], xres.ap()[:, c * Tl:(c + 1) * Tl], nm_bc.ap()))(),
            reads=[xres.name, nm_bc.name], writes=["at_S"])
        P.op("vector", (lambda c=c: lambda v: v.tensor_mul(
            xnbf.ap()[:, c * Tl:(c + 1) * Tl], xsc.ap()[:, 0:Tl], rs_bc.ap()))(),
            reads=["at_S", rs_bc.name], writes=[xnbf.name], tag=f"{tag}_xn{c}")


def emit_xn32(kb, xres, nm_bc, rs_bc, c, out_ap, out_name, engine="vector"):
    P = kb.P
    P.op(engine, lambda v: v.tensor_add(out_ap, xres.ap()[:, c * T:(c + 1) * T], nm_bc.ap()),
         reads=[xres.name, nm_bc.name], writes=[out_name])
    P.op(engine, lambda v: v.tensor_mul(out_ap, out_ap, rs_bc.ap()),
         reads=[out_name, rs_bc.name], writes=[out_name])



def linear_fm(kb, wname, wap, Din, Dout, rhs_chunk, Tn, epilogue, *, tag,
              m0_start=0, mtile=128, nt=512, queue="gpsimd"):
    P = kb.P
    nch = Din // 128
    nmb = cdiv(Dout, mtile)
    assert nch * mtile <= 1024
    pbs = [kb.pb0, kb.pb1]
    for mi in range(nmb):
        m0 = mi * mtile
        mw = min(mtile, Dout - m0)
        wb_ap, wb_n = kb.wslot(mi % 2)
        kb.stream_w(wap, 0, Din, m0_start + m0, mw, wb_ap, wb_n, queue=queue)
        for ni in range(cdiv(Tn, nt)):
            n0 = ni * nt
            nw = min(nt, Tn - n0)
            pb = pbs[(mi * cdiv(Tn, nt) + ni) % 2]
            for kc in range(nch):
                rap, rnames = rhs_chunk(kc)
                P.op("tensor", (lambda kc=kc, mw=mw, nw=nw, n0=n0, wb_ap=wb_ap, pb=pb, rap=rap: lambda t:
                     t.matmul(pb.ap()[0:mw, 0:nw], wb_ap[:, kc * mw:(kc + 1) * mw],
                              rap[:, n0:n0 + nw], start=(kc == 0), stop=(kc == nch - 1)))(),
                     reads=[wb_n] + list(rnames), writes=[pb.name], tag=f"{tag}_mm")
            epilogue(mi, m0, mw, n0, nw, pb.ap()[0:mw, 0:nw], pb.name)



def attn_head(kb, q_ap, q_names, k_ap, k_names, v_ap, Tq, Tk, hd, scale,
              causal, out_write, tag, extra_mask=None, kv_tchunks=None):
    P = kb.P
    Sf, Pb, PT, cols = kb.atS, kb.atP, kb.atPT, kb.atC
    nqt = Tq // 128
    for qt in range(nqt):
        Lk = min(Tk, (qt + 1) * 128) if causal else Tk
        nst = cdiv(Lk, 512)
        for si in range(nst):
            s0 = si * 512
            sw = min(512, Lk - s0)
            P.op("tensor", (lambda qt=qt, s0=s0, sw=sw: lambda t: t.matmul(
                kb.psA.ap()[0:128, s0:s0 + sw], q_ap[:, qt * 128:(qt + 1) * 128],
                k_ap[:, s0:s0 + sw], start=True, stop=True))(),
                reads=list(q_names) + list(k_names), writes=["psA"], tag=f"{tag}_qk")
        P.op("vector", (lambda Lk=Lk: lambda v: v.tensor_copy(
            Sf.ap()[:, 0:Lk], kb.psA.ap()[:, 0:Lk]))(),
            reads=["psA"], writes=["at_S"], tag=f"{tag}_Scp")
        if causal:
            P.op("vector", (lambda qt=qt, Lk=Lk: lambda v: v.tensor_add(
                Sf.ap()[:, qt * 128:Lk], Sf.ap()[:, qt * 128:Lk], kb.cmask.ap()))(),
                reads=["at_S", "c_cmask"], writes=["at_S"], tag=f"{tag}_msk")
        if extra_mask is not None:
            extra_mask(qt, Sf, Lk)
        P.op("vector", (lambda Lk=Lk: lambda v: v.reduce_max(
            cols.ap()[:, 0:1], Sf.ap()[:, 0:Lk], AX.X, negate=True))(),
            reads=["at_S"], writes=["at_cols"])
        P.op("vector", lambda v: v.tensor_scalar_mul(cols.ap()[:, 1:2], cols.ap()[:, 0:1], scale),
             reads=["at_cols"], writes=["at_cols"])
        P.op("scalar", (lambda Lk=Lk: lambda s: s.activation(
            Pb.ap()[:, 0:Lk], Sf.ap()[:, 0:Lk], AF.Exp, bias=cols.ap()[:, 1:2],
            scale=scale, accum_out=cols.ap()[:, 2:3]))(),
            reads=["at_S", "at_cols"], writes=["at_P", "at_cols"], tag=f"{tag}_exp")
        P.op("vector", lambda v: v.reciprocal(cols.ap()[:, 3:4], cols.ap()[:, 2:3]),
             reads=["at_cols"], writes=["at_cols"])
        P.op("vector", (lambda Lk=Lk: lambda v: v.tensor_scalar_mul(
            Pb.ap()[:, 0:Lk], Pb.ap()[:, 0:Lk], cols.ap()[:, 3:4]))(),
            reads=["at_P", "at_cols"], writes=["at_P"], tag=f"{tag}_nrm")
        nsc = cdiv(Lk, 128)
        for g0 in range(0, nsc, 8):
            g1 = min(g0 + 8, nsc)
            for sc in range(g0, g1):
                kb.transpose_b(Pb.ap()[:, sc * 128:min(Lk, (sc + 1) * 128)], sc - g0,
                               fi=min(128, Lk - sc * 128), reads=["at_P"])
            P.op("vector", (lambda g0=g0, g1=g1: lambda v: v.tensor_copy(
                PT.ap()[:, g0 * 128:g1 * 128], kb.pbT.ap()[:, 0:(g1 - g0) * 128]))(),
                reads=["pbT"], writes=["at_PT"], tag=f"{tag}_ptev")
        for sc in range(nsc):
            sw = min(128, Lk - sc * 128)
            vap, vnames = v_ap(sc)
            P.op("tensor", (lambda sc=sc, sw=sw, vap=vap, nsc=nsc: lambda t: t.matmul(
                kb.pbX.ap()[0:hd, 0:128], vap[0:sw, :],
                PT.ap()[0:sw, sc * 128:sc * 128 + 128],
                start=(sc == 0), stop=(sc == nsc - 1)))(),
                reads=list(vnames) + ["at_PT"], writes=["pbX"], tag=f"{tag}_pv")
        out_write(qt, kb.pbX.ap()[0:hd, 0:128], "pbX")



def emit_l3_attn(kb, xres, xnbf):
    P = kb.P
    HD, NH = 64, 16
    qb_ap = kb.arf2.ap()[:, 0:1024].bitcast(BF16)
    qb_n = ("arf2", 0, 1024)
    kb_ap = kb.arf2.ap()[:, 1024:2048].bitcast(BF16)
    kb_n = ("arf2", 1024, 2048)
    vt_ap = kb.arf2.ap()[:, 2048:3072].bitcast(BF16)
    vt_n = ("arf2", 2048, 3072)
    outfm = kb.arb1

    def rhs(kc):
        return xnbf.ap()[:, kc * T:(kc + 1) * T], [xnbf.name]

    for hp in range(8):
        for wkey, dst_ap, dst_n in (("L3_attn_WqT", qb_ap, qb_n),
                                     ("L3_attn_WkT", kb_ap, kb_n)):
            def epi(mi, m0, mw, n0, nw, ps, psn, dst_ap=dst_ap, dst_n=dst_n):
                P.op("vector", lambda v: v.tensor_copy(dst_ap[:, n0:n0 + nw], ps),
                     reads=[psn], writes=[dst_n], tag="l3qk_ev")
            linear_fm(kb, wkey, kb.d[wkey], D, 128, rhs, T, epi,
                      tag="l3qk", m0_start=hp * 128)
        wvb_ap, wvb_n = kb.wslot(2)
        kb.stream_w(kb.d["L3_attn_WvT"], 0, D, hp * 128, 128, wvb_ap, wvb_n)
        for tc in range(16):
            for kc in range(DCH):
                P.op("tensor", (lambda tc=tc, kc=kc: lambda t: t.matmul(
                    kb.pb0.ap()[0:128, 0:128],
                    xnbf.ap()[:, kc * T + tc * 128:kc * T + (tc + 1) * 128],
                    wvb_ap[:, kc * 128:(kc + 1) * 128],
                    start=(kc == 0), stop=(kc == DCH - 1)))(),
                    reads=[xnbf.name, wvb_n], writes=["pb0"], tag="l3v_mm")
            P.op("vector", (lambda tc=tc: lambda v: v.tensor_copy(
                vt_ap[:, tc * 128:(tc + 1) * 128], kb.pb0.ap()[0:128, 0:128]))(),
                reads=["pb0"], writes=[vt_n], tag="l3v_ev")
        for hh in range(2):
            h = hp * 2 + hh
            base = hh * 64

            def vv(sc, base=base):
                return vt_ap[:, sc * 128 + base:sc * 128 + base + 64], [vt_n]

            def ow(qt, ps, psn, h=h, base=base, hp=hp):
                P.op("vector", (lambda qt=qt, ps=ps, base=base, hp=hp: lambda v: v.tensor_copy(
                    outfm.ap()[base:base + 64, hp * T + qt * 128:hp * T + (qt + 1) * 128], ps))(),
                    reads=[psn], writes=["arb1"], tag="l3o_ev")

            attn_head(kb, qb_ap[base:base + 64, :], [qb_n],
                      kb_ap[base:base + 64, :], [kb_n],
                      vv, T, T, HD, 0.125, True, ow, f"l3h{h}")
    def rhs_o(kc):
        return outfm.ap()[:, kc * T:(kc + 1) * T], ["arb1"]

    def epi_o(mi, m0, mw, n0, nw, ps, psn):
        c = m0 // 128
        P.op("vector", lambda v: v.tensor_add(
            xres.ap()[:, c * T + n0:c * T + n0 + nw],
            xres.ap()[:, c * T + n0:c * T + n0 + nw], ps),
            reads=[psn, xres.name], writes=[xres.name], tag="l3wo_ev")
    linear_fm(kb, "L3_attn_WoT", kb.d["L3_attn_WoT"], D, D, rhs_o, T, epi_o, tag="l3wo")


def _amax_partition_fold(kb, col_ap, col_name, out_ap, out_name, slot=0):
    P = kb.P
    P.op("tensor", lambda t: t.matmul(
        kb.pbX.ap()[0:1, 256:384], col_ap,
        kb.identf.ap(), start=True, stop=True, is_transpose=True),
        reads=[col_name, "c_identf"], writes=["pbX"], tag="amax_tp")
    P.op("vector", lambda v: v.reduce_max(
        out_ap, kb.pbX.ap()[0:1, 256:384], AX.X),
        reads=["pbX"], writes=[out_name], tag="amax_red")


MAGIC = 12582912.0


def emit_qdq(kb, out_ap, out_name, in_ap, in_names, inv_ap, s_ap, sc_name, tmp32,
             tmpname, ti32=None, tiname=None, engine="vector"):
    P = kb.P
    P.op(engine, lambda v: v.tensor_scalar(
        tmp32, in_ap, inv_ap, MAGIC, ALU.mult, ALU.add),
        reads=list(in_names) + [sc_name], writes=[tmpname])
    P.op(engine, lambda v: v.tensor_scalar(
        out_ap, tmp32, MAGIC, s_ap, ALU.subtract, ALU.mult),
        reads=[tmpname, sc_name], writes=[out_name])


def emit_l3_ffn(kb, xres, xnbf, stage="all"):
    P = kb.P
    nc = kb.nc
    I3 = 3072
    QA = 32767.0
    ST = stage

    def _sp(nm, kind):
        return nc.dram_tensor(nm, [128, 24 * T], BF16, kind=kind).ap()
    sp_up = sp_gate = sp_p3 = sp_opre = sp_wdnT = None
    if ST == "all":
        sp_up = _sp("sp_up", "Internal")
        sp_gate = _sp("sp_gate", "Internal")
        sp_p3 = _sp("sp_p3", "Internal")
        sp_opre = _sp("sp_opre", "Internal")
        cc_in = nc.dram_tensor("cc_in", [1, 8], F32, kind="Internal").ap()
        cc_out = nc.dram_tensor("cc_out", [1, 8], F32, kind="Internal").ap()
    elif ST == 1:
        sp_up = _sp("sp_up", "ExternalOutput")
        sp_gate = _sp("sp_gate", "ExternalOutput")
    elif ST == 2:
        sp_up = _sp("sp_up", "ExternalInput")
        sp_gate = _sp("sp_gate", "ExternalInput")
        sp_p3 = _sp("sp_p3", "ExternalOutput")
    elif ST == 3:
        sp_up = _sp("sp_up", "ExternalInput")
        sp_p3 = _sp("sp_p3", "ExternalInput")
        sp_opre = _sp("sp_opre", "ExternalOutput")
    elif ST == 4:
        sp_opre = _sp("sp_opre", "ExternalInput")
    if ST in ("all", 4):
        sp_wdnT = nc.dram_tensor("sp_wdnT", [128, 24 * 8 * 128], BF16,
                                 kind="Internal").ap()
    pmax = gmax = None
    if ST != "all":
        pmax = nc.dram_tensor("pmax", [1, 2], F32, kind="ExternalOutput").ap()
        if ST != 0:
            gmax = nc.dram_tensor("gmax", [1, 40], F32, kind="ExternalInput").ap()
    gmx = kb.sc("q_gmx", [1, 40], F32, 0) if gmax is not None else None
    if gmax is not None:
        P.dma("sync", gmx.ap(), gmax[:, :], writes=["q_gmx"])

    def stage_in(pairs):
        for seg, col, i, qm in pairs:
            P.op("vector", (lambda seg=seg, col=col: lambda v: v.reduce_max(
                sca.ap()[:, col:col + 1], gmx.ap()[:, seg * 8:(seg + 1) * 8],
                AX.X))(),
                reads=["q_gmx"], writes=["q_sca"], tag="l3gmx")
            mk_scale(i, qm)

    def stage_out(cols):
        for j, col in enumerate(cols):
            P.dma("sync", pmax[:, j:j + 1], sca.ap()[:, col:col + 1],
                  reads=["q_sca"], writes=["pmax"])

    amx = kb.sc("q_amx", [128, 8], F32, 160)
    sca = kb.sc("q_sca", [1, 16], F32, 192)
    scb = kb.sc("q_scb", [128, 16], F32, 256)
    P.op("vector", lambda v: v.memset(sca.ap(), 0.0), writes=["q_sca"])
    P.op("vector", lambda v: v.memset(amx.ap(), 0.0), writes=["q_amx"])
    t32 = kb.arf2.ap()[:, 0:2048]
    t32n = ("arf2", 0, 2048)
    tC = kb.arf2.ap()[:, 2048:4096]
    tCn = ("arf2", 2048, 4096)
    wtmp = kb.arf2.ap()[:, 0:3072]
    wtmpn = ("arf2", 0, 3072)
    ti32 = kb.atS.ap().bitcast(I32)
    tin = "at_S"
    tA = kb.atP.ap()
    tAn = "at_P"
    tB = kb.atPT.ap()
    tBn = "at_PT"
    evrow = kb.atP.ap()
    evrow_n = "at_P"
    wq_bf_ap = xnbf.ap()[:, 0:I3]
    wq_bf_n = ("xnbf", 0, I3)
    wdnT_ap = kb.arf2.ap()[:, 2048:3584].bitcast(BF16)
    wdnT_n = ("arf2", 2048, 3584)
    wsc = kb.sc("q_wsc", [128, 96 * 2], F32, 320)
    wlhs = [kb.wslot(0), kb.wslot(1)]
    xq = kb.arb1
    xqn = "arb1"

    def allreduce_max(vals_ap, n, tok_reads):
        P.dma("sync", cc_in[:, 0:n], vals_ap, reads=tok_reads, writes=["cc_in"])
        P.op("gpsimd", (lambda n=n: lambda g: g.collective_compute(
            "AllReduce", ALU.max, [list(range(N_CORES))],
            [cc_in[:, 0:n].opt()], [cc_out[:, 0:n].opt()]))(),
            reads=["cc_in"], writes=["cc_out"], tag="cc")
        P.dma("sync", vals_ap, cc_out[:, 0:n], reads=["cc_out"], writes=tok_reads)

    def mk_scale(i, qmax):
        P.op("vector", (lambda i=i: lambda v: v.tensor_scalar(
            sca.ap()[:, 2 * i + 1:2 * i + 2], sca.ap()[:, 2 * i:2 * i + 1],
            1.0 / qmax, 1e-12, ALU.mult, ALU.max))(),
            reads=["q_sca"], writes=["q_sca"])
        P.op("vector", (lambda i=i: lambda v: v.reciprocal(
            sca.ap()[:, 2 * i:2 * i + 1], sca.ap()[:, 2 * i + 1:2 * i + 2]))(),
            reads=["q_sca"], writes=["q_sca"])
        kb.bcast_row(sca.ap()[:, 2 * i:2 * i + 2], 2, scb.ap()[:, 2 * i:2 * i + 2],
                     "q_scb", reads=["q_sca"])

    if ST in ("all", 0):
        for c in range(DCH):
            P.op("vector", (lambda c=c: lambda v: v.tensor_reduce(
                amx.ap()[:, 0:1] if c == 0 else amx.ap()[:, 1:2],
                xnbf.ap()[:, c * T:(c + 1) * T], AX.X, ALU.max, apply_absolute_value=True))(),
                reads=[xnbf.name], writes=["q_amx"], tag="l3f_amx")
            if c > 0:
                P.op("vector", lambda v: v.tensor_max(
                    amx.ap()[:, 0:1], amx.ap()[:, 0:1], amx.ap()[:, 1:2]),
                    reads=["q_amx"], writes=["q_amx"])
        _amax_partition_fold(kb, amx.ap()[:, 0:1], "q_amx", sca.ap()[:, 0:1], "q_sca")
        if ST == 0:
            stage_out([0])
            return
        allreduce_max(sca.ap()[:, 0:1], 1, ["q_sca"])
    if ST in ("all", 1):
        if ST == 1:
            stage_in([(0, 0, 0, QA)])
        else:
            mk_scale(0, QA)
        import os as _os1
        for c in range(DCH if not _os1.environ.get("MK_L3_SKIPXQ") else 0):
            emit_qdq(kb, xq.ap()[:, c * T:(c + 1) * T], xqn,
                     xnbf.ap()[:, c * T:(c + 1) * T], [xnbf.name],
                     scb.ap()[:, 0:1], scb.ap()[:, 1:2], "q_scb",
                     t32, t32n, ti32, tin)

    def dequant_chunk(wdram, row0, ncols, lhsbuf_ap, lhsbuf_n):
        import os as _osd
        _stop = int(_osd.environ.get("MK_L3_DQSTOP", "99"))
        nb = ncols // 32
        P.dma("sync", wtmp[:, 0:ncols], wdram[row0:row0 + 128, 0:ncols],
              writes=[wtmpn])
        if _stop < 2: return
        x3 = wtmp[:, 0:ncols].rearrange("p (a b) -> p a b", b=32)
        P.op("vector", (lambda: lambda v: v.tensor_reduce(
            wsc.ap()[:, 0:nb], x3, AX.X, ALU.max, apply_absolute_value=True))(),
            reads=[wtmpn], writes=["q_wsc"], tag="deq_bm")
        if _stop < 3: return
        P.op("vector", (lambda: lambda v: v.tensor_scalar(
            wsc.ap()[:, 96:96 + nb], wsc.ap()[:, 0:nb], 1.0 / 7.0, 1e-12,
            ALU.mult, ALU.max))(), reads=["q_wsc"], writes=["q_wsc"])
        P.op("vector", (lambda: lambda v: v.reciprocal(
            wsc.ap()[:, 0:nb], wsc.ap()[:, 96:96 + nb]))(),
            reads=["q_wsc"], writes=["q_wsc"])
        if _stop < 4: return
        invb = bAP(wsc.ap()[:, 0:nb], [wsc.ap().ap[0], [1, nb], [0, 32]])
        sb_ = bAP(wsc.ap()[:, 96:96 + nb], [wsc.ap().ap[0], [1, nb], [0, 32]])
        P.op("vector", (lambda: lambda v: v.tensor_tensor(x3, x3, invb, ALU.mult))(),
             reads=[wtmpn, "q_wsc"], writes=[wtmpn])
        if _stop < 5: return
        P.op("vector", (lambda: lambda v: v.tensor_scalar(
            wtmp[:, 0:ncols], wtmp[:, 0:ncols], MAGIC, MAGIC, ALU.add, ALU.subtract))(),
            reads=[wtmpn], writes=[wtmpn])
        if _stop < 6: return
        P.op("vector", (lambda: lambda v: v.tensor_tensor(
            wq_bf_ap[:, 0:ncols].rearrange("p (a b) -> p a b", b=32), x3, sb_, ALU.mult))(),
            reads=[wtmpn, "q_wsc"], writes=[wq_bf_n], tag="deq_s")
        if _stop < 7: return
        ng = ncols // 128
        for g0 in range(0, ng, 8):
            g1 = min(g0 + 8, ng)
            for bi in range(g0, g1):
                kb.transpose_b(wq_bf_ap[:, bi * 128:(bi + 1) * 128], bi - g0,
                               reads=[wq_bf_n])
            P.op("vector", (lambda g0=g0, g1=g1: lambda v: v.tensor_copy(
                lhsbuf_ap[:, g0 * 128:g1 * 128], kb.pbT.ap()[:, 0:(g1 - g0) * 128]))(),
                reads=["pbT"], writes=[lhsbuf_n], tag="deq_tev")

    def qlinear_spill(wdram, spill, amx_slot):
        import os as _os
        for mi in range(int(_os.environ.get("MK_L3_MI", "24"))):
            lb_ap, lb_n = wlhs[mi % 2]
            dequant_chunk(wdram, mi * 128, 1024, lb_ap, lb_n)
            if _os.environ.get("MK_L3_NOMM"):
                continue
            _mms = int(_os.environ.get("MK_L3_MMSTOP", "5"))
            for ni in range(4):
                pb = [kb.pb0, kb.pb1][ni % 2]
                for kc in range(8):
                    P.op("tensor", (lambda mi=mi, ni=ni, kc=kc, lb_ap=lb_ap, pb=pb: lambda t:
                         t.matmul(pb.ap()[0:128, 0:512],
                                  lb_ap[:, kc * 128:(kc + 1) * 128],
                                  xq.ap()[:, kc * T + ni * 512:kc * T + ni * 512 + 512],
                                  start=(kc == 0), stop=(kc == 7)))(),
                         reads=[lb_n, xqn], writes=[pb.name], tag="qlin_mm")
                if _mms < 2: continue
                P.op("vector", (lambda pb=pb: lambda v: v.tensor_reduce(
                    amx.ap()[:, 2:3], pb.ap()[0:128, 0:512], AX.X, ALU.max,
                    apply_absolute_value=True))(),
                    reads=[pb.name], writes=["q_amx"])
                if _mms < 3: continue
                P.op("vector", (lambda amx_slot=amx_slot: lambda v: v.tensor_max(
                    amx.ap()[:, amx_slot:amx_slot + 1], amx.ap()[:, amx_slot:amx_slot + 1],
                    amx.ap()[:, 2:3]))(),
                    reads=["q_amx"], writes=["q_amx"])
                if _mms < 4: continue
                P.op("vector", (lambda ni=ni, pb=pb: lambda v: v.tensor_copy(
                    evrow[:, ni * 512:(ni + 1) * 512], pb.ap()[0:128, 0:512]))(),
                    reads=[pb.name], writes=[evrow_n], tag="qlin_ev")
            if _mms >= 5:
                P.dma("sync", spill[:, mi * T:(mi + 1) * T], evrow,
                      reads=[evrow_n], writes=[spill.tensor.name])

    if ST in ("all", 1):
        import os as _os
        P.op("vector", lambda v: v.memset(amx.ap()[:, 3:5], 0.0), writes=["q_amx"])
        qlinear_spill(kb.d["L3_ffn_w_up"], sp_up, 3)
        if not _os.environ.get("MK_L3_NOGATE"):
            qlinear_spill(kb.d["L3_ffn_w_gate"], sp_gate, 4)
        _amax_partition_fold(kb, amx.ap()[:, 3:4], "q_amx", sca.ap()[:, 2:3], "q_sca", 1)
        _amax_partition_fold(kb, amx.ap()[:, 4:5], "q_amx", sca.ap()[:, 4:5], "q_sca", 2)
        if ST == 1:
            stage_out([2, 4])
            return
        allreduce_max(sca.ap()[:, 2:5], 3, ["q_sca"])
        mk_scale(1, QA)
        mk_scale(2, QA)

    if ST in ("all", 2):
        if ST == 2:
            stage_in([(2, 4, 2, QA)])
        P.op("vector", lambda v: v.memset(amx.ap()[:, 5:6], 0.0), writes=["q_amx"])
        for mi in range(24):
            P.dma("sync", tA, sp_up[:, mi * T:(mi + 1) * T], reads=[sp_up.tensor.name],
                  writes=[tAn])
            P.dma("sync", tB, sp_gate[:, mi * T:(mi + 1) * T], reads=[sp_gate.tensor.name],
                  writes=[tBn])
            emit_qdq(kb, t32, t32n, tB, [tBn], scb.ap()[:, 4:5],
                     scb.ap()[:, 5:6], "q_scb", t32, t32n, ti32, tin)
            P.op("scalar", lambda s: s.activation(tC, t32, AF.Sigmoid),
                 reads=[t32n], writes=[tCn], tag="l3f_sig")
            P.op("vector", lambda v: v.tensor_scalar(
                tC, tC, 65536.0, MAGIC, ALU.mult, ALU.add),
                reads=[tCn], writes=[tCn])
            P.op("vector", lambda v: v.tensor_scalar(
                tC, tC, MAGIC, 65535.0, ALU.subtract, ALU.min),
                reads=[tCn], writes=[tCn])
            P.op("vector", lambda v: v.tensor_scalar_mul(tC, tC, 1.0 / 65536.0),
                reads=[tCn], writes=[tCn])
            P.op("vector", lambda v: v.tensor_mul(tC, tC, t32),
                 reads=[tCn, t32n], writes=[tCn])
            P.op("vector", lambda v: v.tensor_reduce(
                amx.ap()[:, 2:3], tC, AX.X, ALU.max, apply_absolute_value=True),
                reads=[tCn], writes=["q_amx"])
            P.op("vector", lambda v: v.tensor_max(
                amx.ap()[:, 5:6], amx.ap()[:, 5:6], amx.ap()[:, 2:3]),
                reads=["q_amx"], writes=["q_amx"])
            P.dma("gpsimd", sp_p3[:, mi * T:(mi + 1) * T], tC, reads=[tCn],
                  writes=[sp_p3.tensor.name])
        _amax_partition_fold(kb, amx.ap()[:, 5:6], "q_amx", sca.ap()[:, 6:7], "q_sca", 3)
        if ST == 2:
            stage_out([6])
            return
        allreduce_max(sca.ap()[:, 6:7], 1, ["q_sca"])
        mk_scale(3, QA)

    if ST in ("all", 3):
        if ST == 3:
            stage_in([(1, 2, 1, QA), (3, 6, 3, QA)])
        P.op("vector", lambda v: v.memset(amx.ap()[:, 5:6], 0.0), writes=["q_amx"])
        for mi in range(24):
            P.dma("sync", tA, sp_up[:, mi * T:(mi + 1) * T], reads=[sp_up.tensor.name],
                  writes=[tAn])
            P.dma("sync", tB, sp_p3[:, mi * T:(mi + 1) * T], reads=[sp_p3.tensor.name],
                  writes=[tBn])
            emit_qdq(kb, t32, t32n, tA, [tAn], scb.ap()[:, 2:3],
                     scb.ap()[:, 3:4], "q_scb", t32, t32n, ti32, tin)
            emit_qdq(kb, tC, tCn, tB, [tBn], scb.ap()[:, 6:7],
                     scb.ap()[:, 7:8], "q_scb", tC, tCn, ti32, tin)
            P.op("vector", lambda v: v.tensor_mul(tC, tC, t32),
                 reads=[tCn, t32n], writes=[tCn])
            P.op("vector", lambda v: v.tensor_reduce(
                amx.ap()[:, 2:3], tC, AX.X, ALU.max, apply_absolute_value=True),
                reads=[tCn], writes=["q_amx"])
            P.op("vector", lambda v: v.tensor_max(
                amx.ap()[:, 5:6], amx.ap()[:, 5:6], amx.ap()[:, 2:3]),
                reads=["q_amx"], writes=["q_amx"])
            P.dma("gpsimd", sp_opre[:, mi * T:(mi + 1) * T], tC, reads=[tCn],
                  writes=[sp_opre.tensor.name])
        _amax_partition_fold(kb, amx.ap()[:, 5:6], "q_amx", sca.ap()[:, 8:9], "q_sca", 4)
        if ST == 3:
            stage_out([8])
            return
        allreduce_max(sca.ap()[:, 8:9], 1, ["q_sca"])
        mk_scale(4, QA)

    if ST == 4:
        stage_in([(4, 8, 4, QA)])
    for m in range(8):
        dequant_chunk(kb.d["L3_ffn_w_down"], m * 128, 3072, wdnT_ap, wdnT_n)
        P.dma("sync", sp_wdnT[:, m * 24 * 128:(m + 1) * 24 * 128],
              wdnT_ap[:, 0:24 * 128],
              reads=[wdnT_n], writes=[sp_wdnT.tensor.name])

    ostrip = kb.arb1
    for ni in range(4):
        n0 = ni * 512
        for kc in range(24):
            P.dma("sync", tA[:, 0:512], sp_opre[:, kc * T + n0:kc * T + n0 + 512],
                  reads=[sp_opre.tensor.name], writes=[tAn])
            emit_qdq(kb, ostrip.ap()[:, kc * 512:(kc + 1) * 512], "arb1",
                     tA[:, 0:512], [tAn], scb.ap()[:, 8:9],
                     scb.ap()[:, 9:10], "q_scb", t32[:, 0:512], t32n,
                     ti32[:, 0:512], tin)
        for m in range(8):
            P.dma("sync", wdnT_ap[:, 0:24 * 128],
                  sp_wdnT[:, m * 24 * 128:(m + 1) * 24 * 128],
                  reads=[sp_wdnT.tensor.name], writes=[wdnT_n])
            for kc in range(24):
                P.op("tensor", (lambda m=m, kc=kc, ni=ni: lambda t: t.matmul(
                    kb.pb0.ap()[0:128, 0:512], wdnT_ap[:, kc * 128:(kc + 1) * 128],
                    ostrip.ap()[:, kc * 512:(kc + 1) * 512],
                    start=(kc == 0), stop=(kc == 23)))(),
                    reads=[wdnT_n, "arb1"], writes=["pb0"], tag="l3dn_mm")
            P.op("vector", (lambda m=m, n0=n0: lambda v: v.tensor_add(
                xres.ap()[:, m * T + n0:m * T + n0 + 512],
                xres.ap()[:, m * T + n0:m * T + n0 + 512], kb.pb0.ap()[0:128, 0:512]))(),
                reads=["pb0", xres.name], writes=[xres.name], tag="l3dn_ev")



def emit_l0(kb, xres, xnbf, nm_bc, rs_bc):
    kb.ensure_iota()
    """Full layer 0. xnbf buffer is used as xn TOKEN-major here."""
    P = kb.P
    nc = kb.nc
    xntm = xnbf
    NEX, HID, CAP = 8, 4096, 513
    JPAD = 640
    sp_xc = nc.dram_tensor("sp_xc", [128, DCH * T], BF16, kind="Internal").ap()
    sp_dt = nc.dram_tensor("sp_dt", [128, DCH * T], BF16, kind="Internal").ap()

    convw = kb.sc("l0_convw", [128, 32], F32, 20480)
    convb = kb.sc("l0_convb", [128, 8], F32, 20608)
    dcol = kb.sc("l0_dcol", [128, 8], F32, 20640)
    dtb = kb.sc("l0_dtb", [128, 8], F32, 20672)
    Acol = kb.sc("l0_A", [128, 8 * 16], F32, 20704)
    class _V:
        def __init__(self, ap):
            self._ap = ap
        def ap(self):
            return self._ap
    dtr1 = _V(kb.warena.ap()[0:64, 1024:3072].bitcast(BF16))
    dtr1_n = ("warena", 1024, 3072)
    Bfm = kb.sc("l0_B", [16, T], BF16, 0)
    Cfm = kb.sc("l0_C", [16, T], BF16, 4096)
    sel16 = kb.sc("l0_sel16", [16, 16 * 128 + 1024], BF16, 8192)
    sct = _V(kb.arf2.ap()[:, 2048:2176])
    sct_n = ("arf2", 2048, 2176)
    ptm = kb.sc("l0_ptm", [128, 128], F32, 21216)
    gtm = kb.sc("l0_gtm", [128, 128], F32, 21728)
    b2r = _V(sel16.ap()[0:8, 2048:3072])
    b1c = kb.sc("l0_b1c", [128, 32], F32, 22240)
    xself = kb.sc("l0_xselF", [128, 8 * 384], BF16, 14336)
    ytm = _V(kb.arf2.ap()[:, 0:2048].bitcast(BF16))
    ytm_n = ("arf2", 0, 2048)
    b2bc = _V(kb.warena.ap()[:, 0:1024])
    b2bc_n = ("warena", 0, 1024)
    Bnbc = kb.atP
    Cnbc = kb.atPT
    xcld = _V(kb.atS.ap()[:, 0:1024].bitcast(BF16))
    xcld_n = ("at_S", 0, 1024)
    dtld = _V(kb.atS.ap()[:, 1024:2048].bitcast(BF16))
    dtld_n = ("at_S", 1024, 2048)

    P.dma("sync", convw.ap(), kb.d["L0_convW"][:, :].rearrange("(c p) m -> p c m", p=128),
          writes=["l0_convw"])
    P.dma("sync", convb.ap(), kb.d["L0_convBT"][:, :], writes=["l0_convb"])
    P.dma("sync", dcol.ap(), kb.d["L0_DT"][:, :], writes=["l0_dcol"])
    P.dma("sync", dtb.ap(), kb.d["L0_dtprojBT"][:, :], writes=["l0_dtb"])
    P.dma("gpsimd", sel16.ap()[:, 0:2048], kb.d["C_selall"][:, :], writes=["l0_sel16"])
    P.dma("gpsimd", b2r.ap(), kb.d["L0_ffn_b2"][:, :], writes=["l0_sel16"])
    P.dma("sync", Acol.ap(), kb.d["L0_attn_A_log"][:, :].rearrange("(c p) n -> p c n", p=128),
          writes=["l0_A"])
    P.op("scalar", lambda s: s.activation(Acol.ap(), Acol.ap(), AF.Exp),
         reads=["l0_A"], writes=["l0_A"])
    P.op("vector", lambda v: v.tensor_scalar_mul(Acol.ap(), Acol.ap(), -1.0),
         reads=["l0_A"], writes=["l0_A"])

    xcbf = kb.arb1
    cacc = kb.arf2

    for c in range(DCH):
        emit_xn32(kb, xres, nm_bc, rs_bc, c, kb.atS.ap(), "at_S")
        P.op("vector", (lambda c=c: lambda v: v.tensor_scalar(
            cacc.ap()[:, 0:T], kb.atS.ap(), convw.ap()[:, c * 4 + 3:c * 4 + 4],
            convb.ap()[:, c:c + 1], ALU.mult, ALU.add))(),
            reads=["at_S", "l0_convw", "l0_convb"], writes=[("arf2", 0, 2048)],
            tag="l0conv")
        for j in range(3):
            sh = 3 - j
            P.op("vector", (lambda c=c, j=j, sh=sh: lambda v: v.scalar_tensor_tensor(
                cacc.ap()[:, sh:T], kb.atS.ap()[:, 0:T - sh],
                convw.ap()[:, c * 4 + j:c * 4 + j + 1],
                cacc.ap()[:, sh:T], ALU.mult, ALU.add))(),
                reads=["at_S", "l0_convw", ("arf2", 0, 2048)],
                writes=[("arf2", 0, 2048)], tag="l0conv")
        P.op("scalar", (lambda c=c: lambda s: s.activation(
            xcbf.ap()[:, c * T:(c + 1) * T], cacc.ap()[:, 0:T], AF.Silu))(),
            reads=[("arf2", 0, 2048)], writes=[("arb1", c * T, (c + 1) * T)],
            tag="l0silu")
        P.dma("sync", sp_xc[:, c * T:(c + 1) * T], xcbf.ap()[:, c * T:(c + 1) * T],
              reads=[("arb1", c * T, (c + 1) * T)], writes=[sp_xc.tensor.name])
        P.op("vector", lambda v: v.tensor_copy(kb.atP.ap(), kb.atS.ap()),
             reads=["at_S"], writes=["at_P"])
        for g0 in range(0, 16, 8):
            for tt in range(g0, g0 + 8):
                kb.transpose_b(kb.atP.ap()[:, tt * 128:(tt + 1) * 128], tt - g0,
                               reads=["at_P"])
            P.op("vector", (lambda c=c, g0=g0: lambda v: v.tensor_copy(
                xntm.ap().rearrange("p (a b) -> p a b", b=1024)[:, g0:g0 + 8,
                                                               c * 128:(c + 1) * 128],
                kb.pbT.ap().rearrange("p (a b) -> p a b", b=128)))(),
                reads=["pbT"], writes=[xntm.name], tag="l0tm_ev")
    wv_ap, wv_n = kb.wslot(0)
    kb.stream_w(kb.d["L0_xprojT"], 0, D, 0, 64, wv_ap, wv_n)
    for ni in range(4):
        for kc in range(DCH):
            P.op("tensor", (lambda ni=ni, kc=kc, wv_ap=wv_ap: lambda t: t.matmul(
                kb.pb0.ap()[0:64, 0:512], wv_ap[:, kc * 64:(kc + 1) * 64],
                xcbf.ap()[:, kc * T + ni * 512:kc * T + ni * 512 + 512],
                start=(kc == 0), stop=(kc == DCH - 1)))(),
                reads=[wv_n, "arb1"], writes=["pb0"], tag="l0dtr1_mm")
        P.op("vector", (lambda ni=ni: lambda v: v.tensor_copy(
            dtr1.ap()[:, ni * 512:(ni + 1) * 512], kb.pb0.ap()[0:64, 0:512]))(),
            reads=["pb0"], writes=[dtr1_n], tag="l0dtr1_ev")
    for wname, dst, dname in (("L0_BprojT", Bfm, "l0_B"), ("L0_CprojT", Cfm, "l0_C")):
        wb_ap, wb_n = kb.wslot(0)
        kb.stream_w(kb.d[wname], 0, D, 0, 16, wb_ap, wb_n)
        for ni in range(4):
            for kc in range(DCH):
                P.op("tensor", (lambda ni=ni, kc=kc, wb_ap=wb_ap: lambda t: t.matmul(
                    kb.pb1.ap()[0:16, 0:512], wb_ap[:, kc * 16:(kc + 1) * 16],
                    xcbf.ap()[:, kc * T + ni * 512:kc * T + ni * 512 + 512],
                    start=(kc == 0), stop=(kc == DCH - 1)))(),
                    reads=[wb_n, "arb1"], writes=["pb1"], tag="l0bc_mm")
            P.op("vector", (lambda ni=ni, dst=dst, dname=dname: lambda v: v.tensor_copy(
                dst.ap()[:, ni * 512:(ni + 1) * 512], kb.pb1.ap()[0:16, 0:512]))(),
                reads=["pb1"], writes=[dname], tag="l0bc_ev")
    wd_ap, wd_n = kb.wslot(0)
    P.dma("gpsimd", wd_ap[0:64, 0:1024], kb.d["L0_dtprojT"][:, :], writes=[wd_n])
    for c in range(DCH):
        for ni in range(4):
            P.op("tensor", (lambda c=c, ni=ni, wd_ap=wd_ap: lambda t: t.matmul(
                kb.pbX.ap()[0:128, 0:512], wd_ap[0:64, c * 128:(c + 1) * 128],
                dtr1.ap()[:, ni * 512:(ni + 1) * 512], start=True, stop=True))(),
                reads=[wd_n, dtr1_n], writes=["pbX"], tag="l0dt_mm")
            P.op("vector", (lambda c=c, ni=ni: lambda v: v.tensor_scalar(
                kb.atS.ap()[:, 0:512], kb.pbX.ap()[0:128, 0:512],
                dtb.ap()[:, c:c + 1], 0.0, ALU.add, ALU.min))(),
                reads=["pbX", "l0_dtb"], writes=["at_S"], tag="l0sp")
            P.op("scalar", (lambda: lambda s: s.activation(
                kb.atS.ap()[:, 0:512], kb.atS.ap()[:, 0:512], AF.Exp))(),
                reads=["at_S"], writes=["at_S"])
            P.op("scalar", (lambda: lambda s: s.activation(
                kb.atS.ap()[:, 0:512], kb.atS.ap()[:, 0:512], AF.Ln, bias=1.0))(),
                reads=["at_S"], writes=["at_S"])
            P.op("vector", (lambda c=c, ni=ni: lambda v: v.tensor_scalar(
                kb.atS.ap()[:, 512:1024], kb.pbX.ap()[0:128, 0:512],
                dtb.ap()[:, c:c + 1], 0.0, ALU.add, ALU.max))(),
                reads=["pbX", "l0_dtb"], writes=["at_S"], tag="l0sp2")
            P.op("vector", (lambda c=c, ni=ni: lambda v: v.tensor_add(
                dtld.ap()[:, ni * 512:(ni + 1) * 512], kb.atS.ap()[:, 0:512],
                kb.atS.ap()[:, 512:1024]))(),
                reads=["at_S"], writes=[dtld_n], tag="l0sp3")
        P.dma("sync", sp_dt[:, c * T:(c + 1) * T], dtld.ap(),
              reads=[dtld_n], writes=[sp_dt.tensor.name])
    rw = _V(kb.arf2.ap()[:, 2176:2240])
    rw_n = ("arf2", 2176, 2240)
    kb.stream_w32(kb.d["L0_routerT"], 0, D, 0, 8, rw.ap(), rw_n)
    for kc in range(DCH):
        emit_xn32(kb, xres, nm_bc, rs_bc, kc, kb.atS.ap(), "at_S")
        for tt in range(16):
            P.op("tensor", (lambda tt=tt, kc=kc: lambda t: t.matmul(
                kb.pbX.ap()[0:128, tt * 8:(tt + 1) * 8],
                kb.atS.ap()[:, tt * 128:(tt + 1) * 128],
                rw.ap()[:, kc * 8:(kc + 1) * 8],
                start=(kc == 0 and tt == 0), stop=(kc == DCH - 1 and tt == 15),
                skip_group_check=True))(),
                reads=["at_S", rw_n], writes=["pbX"], tag="l0r_mm")
    l3 = sct.ap().rearrange("p (a b) -> p a b", b=8)
    P.op("vector", lambda v: v.tensor_reduce(
        ptm.ap()[:, 0:16], kb.pbX.ap()[0:128, 0:128].rearrange("p (a b) -> p a b", b=8),
        AX.X, ALU.max, negate=True), reads=["pbX"], writes=["l0_ptm"], tag="l0r_max")
    P.op("vector", lambda v: v.tensor_tensor(
        l3, kb.pbX.ap()[0:128, 0:128].rearrange("p (a b) -> p a b", b=8),
        bAP(ptm.ap(), [ptm.ap().ap[0], [1, 16], [0, 8]]), ALU.add),
        reads=["pbX", "l0_ptm"], writes=[sct_n], tag="l0r_sub")
    P.op("scalar", lambda s: s.activation(sct.ap(), sct.ap(), AF.Exp),
         reads=[sct_n], writes=[sct_n])
    P.op("vector", lambda v: v.tensor_reduce(ptm.ap()[:, 0:16], l3, AX.X, ALU.add),
         reads=[sct_n], writes=["l0_ptm"])
    P.op("vector", lambda v: v.reciprocal(ptm.ap()[:, 0:16], ptm.ap()[:, 0:16]),
         reads=["l0_ptm"], writes=["l0_ptm"])
    P.op("vector", lambda v: v.tensor_tensor(
        l3, l3, bAP(ptm.ap(), [ptm.ap().ap[0], [1, 16], [0, 8]]), ALU.mult),
        reads=[sct_n, "l0_ptm"], writes=[sct_n], tag="l0r_nrm")
    sfm = kb.nm_bc_t.ap()[0:8, 0:T]
    sfm_n = "nm_bc"
    for tt in range(16):
        P.op("tensor", (lambda tt=tt: lambda t: t.matmul(
            kb.pb0.ap()[0:8, 0:128], sct.ap()[:, tt * 8:(tt + 1) * 8],
            kb.identf.ap(), start=True, stop=True, is_transpose=True))(),
            reads=[sct_n, "c_identf"], writes=["pb0"], tag="l0sT_mm")
        P.op("vector", (lambda tt=tt: lambda v: v.tensor_copy(
            sfm[:, tt * 128:(tt + 1) * 128], kb.pb0.ap()[0:8, 0:128]))(),
            reads=["pb0"], writes=[sfm_n], tag="l0sT_ev")
    BI = kb.sc("l0_bi", [8, 16], I32, 22368)
    BF = BI.ap().bitcast(F32)
    mrow = kb.rs_bc_t.ap()[0:8, 0:T]
    mrow_n = "rs_bc"
    sfm_i = sfm.bitcast(I32)
    P.op("vector", lambda v: v.memset(BF[:, 0:1], 0.0), writes=["l0_bi"])
    P.op("vector", lambda v: v.memset(BF[:, 1:2], 1.0), writes=["l0_bi"])
    for it in range(31):
        P.op("vector", lambda v: v.tensor_add(BI.ap()[:, 2:3], BI.ap()[:, 0:1],
             BI.ap()[:, 1:2]), reads=["l0_bi"], writes=["l0_bi"])
        P.op("vector", lambda v: v.tensor_single_scalar(BI.ap()[:, 2:3], BI.ap()[:, 2:3],
             1, ALU.arith_shift_right), reads=["l0_bi"], writes=["l0_bi"])
        P.op("vector", lambda v: v.tensor_scalar(
            mrow, sfm, BF[:, 2:3], None, ALU.is_ge),
            reads=[sfm_n, "l0_bi"], writes=[mrow_n], tag="l0bis_ge")
        P.op("vector", lambda v: v.tensor_reduce(BF[:, 3:4], mrow, AX.X, ALU.add),
             reads=[mrow_n], writes=["l0_bi"])
        P.op("vector", lambda v: v.tensor_single_scalar(BF[:, 4:5], BF[:, 3:4],
             float(CAP), ALU.is_ge), reads=["l0_bi"], writes=["l0_bi"])
        P.op("vector", lambda v: v.tensor_copy(BI.ap()[:, 8:9], BF[:, 4:5]),
             reads=["l0_bi"], writes=["l0_bi"])
        P.op("vector", lambda v: v.copy_predicated(BI.ap()[:, 0:1], BI.ap()[:, 8:9],
             BI.ap()[:, 2:3]), reads=["l0_bi"], writes=["l0_bi"])
        P.op("vector", lambda v: v.tensor_scalar(BF[:, 5:6], BF[:, 4:5], -1.0, 1.0,
             ALU.mult, ALU.add), reads=["l0_bi"], writes=["l0_bi"])
        P.op("vector", lambda v: v.tensor_copy(BI.ap()[:, 9:10], BF[:, 5:6]),
             reads=["l0_bi"], writes=["l0_bi"])
        P.op("vector", lambda v: v.copy_predicated(BI.ap()[:, 1:2], BI.ap()[:, 9:10],
             BI.ap()[:, 2:3]), reads=["l0_bi"], writes=["l0_bi"])
    P.op("vector", lambda v: v.tensor_scalar(
        mrow, sfm, BF[:, 0:1], None, ALU.is_gt),
        reads=[sfm_n, "l0_bi"], writes=[mrow_n], tag="l0_gt")
    P.op("vector", lambda v: v.tensor_reduce(BF[:, 5:6], mrow, AX.X, ALU.add),
         reads=[mrow_n], writes=["l0_bi"])
    P.op("vector", lambda v: v.tensor_scalar(BF[:, 6:7], BF[:, 5:6], -1.0, float(CAP),
         ALU.mult, ALU.add), reads=["l0_bi"], writes=["l0_bi"])
    eqrow = kb.atS.ap()[0:8, 0:T]
    P.op("vector", lambda v: v.tensor_scalar(
        eqrow, sfm, BF[:, 0:1], None, ALU.is_equal),
        reads=[sfm_n, "l0_bi"], writes=["at_S"], tag="l0_eq")
    cums = kb.arf2.ap()[0:8, 0:T]
    P.op("vector", lambda v: v.memset(BF[:, 7:8], 0.0), writes=["l0_bi"])
    P.op("vector", lambda v: v.tensor_tensor_scan(
        cums, eqrow, bAP(BF[:, 7:8], [BF.ap[0], [0, T]]), 0.0, ALU.add, ALU.add),
        reads=["at_S", "l0_bi"], writes=[("arf2", 0, 2048)], tag="l0_cum")
    P.op("vector", lambda v: v.tensor_scalar(
        cums, cums, BF[:, 6:7], None, ALU.is_le),
        reads=[("arf2", 0, 2048), "l0_bi"], writes=[("arf2", 0, 2048)])
    P.op("vector", lambda v: v.tensor_mul(eqrow, eqrow, cums),
         reads=["at_S", ("arf2", 0, 2048)], writes=["at_S"])
    P.op("vector", lambda v: v.tensor_add(mrow, mrow, eqrow),
         reads=[mrow_n, "at_S"], writes=[mrow_n], tag="l0_msk")
    posr = kb.atS.ap()[0:8, 0:T]
    P.op("vector", lambda v: v.tensor_tensor_scan(
        posr, mrow, bAP(BF[:, 7:8], [BF.ap[0], [0, T]]), 0.0, ALU.add, ALU.add),
        reads=[mrow_n, "l0_bi"], writes=["at_S"], tag="l0_pos")
    P.op("vector", lambda v: v.tensor_mul(posr, posr, mrow),
         reads=["at_S", mrow_n], writes=["at_S"])
    gater = kb.arf2.ap()[0:8, 0:T]
    P.op("vector", lambda v: v.tensor_mul(gater, mrow, sfm),
         reads=[mrow_n, sfm_n], writes=[("arf2", 0, 2048)], tag="l0_gate")
    gateb = kb.atP.ap()[0:8, 0:T]
    P.op("vector", lambda v: v.tensor_copy(gateb, gater),
         reads=[("arf2", 0, 2048)], writes=["at_P"])
    for tt in range(16):
        P.op("tensor", (lambda tt=tt: lambda t: t.matmul(
            kb.pb0.ap()[0:128, 0:8], posr[:, tt * 128:(tt + 1) * 128],
            kb.identf.ap()[0:8, 0:8], start=True, stop=True, is_transpose=True))(),
            reads=["at_S", "c_identf"], writes=["pb0"], tag="l0pT_mm")
        P.op("vector", (lambda tt=tt: lambda v: v.tensor_copy(
            ptm.ap()[:, tt * 8:(tt + 1) * 8], kb.pb0.ap()[0:128, 0:8]))(),
            reads=["pb0"], writes=["l0_ptm"], tag="l0pT_ev")
        P.op("tensor", (lambda tt=tt: lambda t: t.transpose(
            kb.pbT.ap()[0:128, 0:8], gateb[:, tt * 128:(tt + 1) * 128],
            kb.ident.ap()[0:8, 0:8]))(),
            reads=["at_P", "c_ident"], writes=["pbT"], tag="l0gT_mm")
        P.op("vector", (lambda tt=tt: lambda v: v.tensor_copy(
            gtm.ap()[:, tt * 8:(tt + 1) * 8], kb.pbT.ap()[0:128, 0:8]))(),
            reads=["pbT"], writes=["l0_gtm"], tag="l0gT_ev")
    for c in range(DCH):
        P.op("vector", (lambda c=c: lambda v: v.scalar_tensor_tensor(
            xres.ap()[:, c * T:(c + 1) * T], xcbf.ap()[:, c * T:(c + 1) * T],
            dcol.ap()[:, c:c + 1], xres.ap()[:, c * T:(c + 1) * T],
            ALU.mult, ALU.add))(),
            reads=[("arb1", c * T, (c + 1) * T), "l0_dcol", xres.name],
            writes=[xres.name], tag="l0_dterm")

    JT2S = ((0, 384), (384, 256))

    def emit_expert(e):
        oh_ap = kb.warena.ap()[:, 2048:2432]
        oh_n = ("warena", 2048, 2432)
        P.dma("sync", b1c.ap(), kb.d["L0_ffn_b1T"][e], writes=["l0_b1c"])
        for dh in range(2):
            P.op("tensor", (lambda e=e, dh=dh: lambda t: t.matmul(
                kb.pbX.ap()[0:128, 0:512], sel16.ap()[0:8, e * 128:e * 128 + 128],
                b2r.ap()[:, dh * 512:(dh + 1) * 512], start=True, stop=True))(),
                reads=["l0_sel16", "l0_sel16"], writes=["pbX"], tag="l0b2_mm")
            P.op("vector", (lambda dh=dh: lambda v: v.tensor_copy(
                b2bc.ap()[:, dh * 512:(dh + 1) * 512], kb.pbX.ap()[0:128, 0:512]))(),
                reads=["pbX"], writes=[b2bc_n], tag="l0b2_ev")
        for jti, (joff, jw) in enumerate(JT2S):
            for dc in range(DCH):
                nsp = cdiv(jw, 512)
                for tc in range(16):
                    P.op("vector", (lambda e=e, joff=joff, jw=jw, tc=tc:
                         lambda v: v.tensor_scalar(
                        oh_ap[:, 0:jw], kb.iota_row.ap()[:, joff:joff + jw],
                        1.0, ptm.ap()[:, tc * 8 + e:tc * 8 + e + 1],
                        ALU.add, ALU.is_equal))(),
                        reads=["c_iota", "l0_ptm"], writes=[oh_n], tag="l0oh")
                    for sp in range(nsp):
                        s0, sw = sp * 512, min(512, jw - sp * 512)
                        P.op("tensor", (lambda dc=dc, tc=tc, sp=sp, s0=s0, sw=sw:
                             lambda t: t.matmul(
                            [kb.pb0, kb.pb1][sp].ap()[0:128, 0:sw],
                            xntm.ap()[:, tc * 1024 + dc * 128:tc * 1024 + (dc + 1) * 128],
                            oh_ap[:, s0:s0 + sw],
                            start=(tc == 0), stop=(tc == 15)))(),
                            reads=[oh_n, xntm.name],
                            writes=[[kb.pb0, kb.pb1][sp].name], tag="l0ga_mm")
                for sp in range(nsp):
                    s0, sw = sp * 512, min(512, jw - sp * 512)
                    P.op("vector", (lambda dc=dc, jw=jw, s0=s0, sw=sw, sp=sp:
                         lambda v: v.tensor_copy(
                        xself.ap()[:, dc * jw + s0:dc * jw + s0 + sw],
                        [kb.pb0, kb.pb1][sp].ap()[0:128, 0:sw]))(),
                        reads=[[kb.pb0, kb.pb1][sp].name], writes=["l0_xselF"],
                        tag="l0ga_ev")
            for hc in range(32):
                w1_ap, w1_n = kb.wslot(1 if hc % 2 == 0 else 2)
                kb.stream_w(kb.d["L0_ffn_w1"][e], 0, D, hc * 128, 128, w1_ap, w1_n)
                pb = [kb.pb0, kb.pb1][hc % 2]
                for kc in range(DCH):
                    P.op("tensor", (lambda hc=hc, kc=kc, jw=jw,
                         w1_ap=w1_ap, pb=pb: lambda t: t.matmul(
                        pb.ap()[0:128, 0:jw], w1_ap[:, kc * 128:(kc + 1) * 128],
                        xself.ap()[:, kc * jw:kc * jw + jw],
                        start=(kc == 0), stop=(kc == DCH - 1)))(),
                        reads=[w1_n, "l0_xselF"], writes=[pb.name], tag="l0h_mm")
                P.op("scalar", (lambda hc=hc, jw=jw, pb=pb: lambda s: s.activation(
                    kb.arb1.ap()[:, hc * jw:(hc + 1) * jw], pb.ap()[0:128, 0:jw],
                    AF.Gelu, bias=b1c.ap()[:, hc:hc + 1]))(),
                    reads=[pb.name, "l0_b1c"],
                    writes=[("arb1", hc * jw, (hc + 1) * jw)], tag="l0gelu")
            njs = jw // 128
            pslots = [(kb.psA, 0), (kb.psA, 512), (kb.psA, 1024), (kb.psA, 1536),
                      (kb.pb0, 0), (kb.pb1, 0)]
            for dh in range(2):
                for hc in range(32):
                    wy_ap, wy_n = kb.wslot(1 if hc % 2 == 0 else 2)
                    kb.stream_w(kb.d["L0_ffn_w2"][e], hc * 128, 128, dh * 512, 512,
                                wy_ap, wy_n)
                    for jsub in range(njs):
                        pst, poff = pslots[jsub]
                        P.op("tensor", (lambda hc=hc, jsub=jsub, dh=dh, jw=jw,
                             wy_ap=wy_ap, pst=pst, poff=poff: lambda t: t.matmul(
                            pst.ap()[0:128, poff:poff + 512],
                            kb.arb1.ap()[:, hc * jw + jsub * 128:hc * jw + jsub * 128 + 128],
                            wy_ap[:, 0:512],
                            start=(hc == 0), stop=(hc == 31)))(),
                            reads=[wy_n, ("arb1", hc * jw + jsub * 128,
                                          hc * jw + jsub * 128 + 128)],
                            writes=[pst.name], tag="l0y_mm")
                for jsub in range(njs):
                    pst, poff = pslots[jsub]
                    dst = ytm.ap()[:, jsub * 1024 + dh * 512:jsub * 1024 + dh * 512 + 512]
                    P.op("vector", (lambda dst=dst, dh=dh, pst=pst, poff=poff:
                         lambda v: v.tensor_add(
                        dst, pst.ap()[0:128, poff:poff + 512],
                        b2bc.ap()[:, dh * 512:(dh + 1) * 512]))(),
                        reads=[pst.name, b2bc_n], writes=[ytm_n], tag="l0y_ev")
            for jsub in range(njs):
                jt = joff // 128 + jsub
                for g0 in range(0, 16, 8):
                    for tc in range(g0, g0 + 8):
                        P.op("vector", (lambda e=e, jt=jt, tc=tc: lambda v: v.tensor_scalar(
                            oh_ap[:, 0:128], kb.iota_row.ap()[:, jt * 128:(jt + 1) * 128],
                            1.0, ptm.ap()[:, tc * 8 + e:tc * 8 + e + 1],
                            ALU.add, ALU.is_equal))(),
                            reads=["c_iota", "l0_ptm"], writes=[oh_n], tag="l0ohg")
                        P.op("vector", (lambda e=e, tc=tc: lambda v: v.tensor_scalar_mul(
                            oh_ap[:, 0:128], oh_ap[:, 0:128],
                            gtm.ap()[:, tc * 8 + e:tc * 8 + e + 1]))(),
                            reads=[oh_n, "l0_gtm"], writes=[oh_n])
                        kb.transpose_b(oh_ap[:, 0:128], tc - g0, reads=[oh_n])
                    P.op("vector", (lambda jsub=jsub, g0=g0: lambda v: v.tensor_copy(
                        kb.arb1.ap()[:, jsub * 2048 + g0 * 128:jsub * 2048 + (g0 + 8) * 128],
                        kb.pbT.ap()))(),
                        reads=["pbT"], writes=[("arb1", jsub * 2048, (jsub + 1) * 2048)],
                        tag="l0ohgT_ev")
            for dc in range(DCH):
                for ttl in range(4):
                    for jsub in range(njs):
                        P.op("tensor", (lambda dc=dc, ttl=ttl, jsub=jsub, njs=njs:
                             lambda t: t.matmul(
                            kb.pbX.ap()[0:128, 0:512],
                            ytm.ap()[:, jsub * 1024 + dc * 128:jsub * 1024 + (dc + 1) * 128],
                            kb.arb1.ap()[:, jsub * 2048 + ttl * 512:jsub * 2048 + ttl * 512 + 512],
                            start=(jsub == 0), stop=(jsub == njs - 1)))(),
                            reads=[ytm_n, ("arb1", jsub * 2048, (jsub + 1) * 2048)],
                            writes=["pbX"], tag="l0sc_mm")
                    P.op("vector", (lambda dc=dc, ttl=ttl: lambda v: v.tensor_add(
                        xres.ap()[:, dc * T + ttl * 512:dc * T + ttl * 512 + 512],
                        xres.ap()[:, dc * T + ttl * 512:dc * T + ttl * 512 + 512],
                        kb.pbX.ap()[0:128, 0:512]))(),
                        reads=["pbX", xres.name], writes=[xres.name], tag="l0sc_ev")

    def emit_scan_n(n):
        for which, dst, dn, srcf, srcn in ((0, Bnbc, "at_P", Bfm, "l0_B"),
                                           (1, Cnbc, "at_PT", Cfm, "l0_C")):
            for si in range(4):
                P.op("tensor", (lambda n=n, si=si, srcf=srcf, srcn=srcn: lambda t:
                     t.matmul(kb.psA.ap()[0:128, si * 512:(si + 1) * 512],
                              sel16.ap()[0:16, n * 128:n * 128 + 128],
                              srcf.ap()[:, si * 512:(si + 1) * 512],
                              start=True, stop=True))(),
                    reads=["l0_sel16", srcn], writes=["psA"], tag="l0bc2_mm")
            P.op("vector", (lambda dst=dst, dn=dn: lambda v: v.tensor_copy(
                dst.ap(), kb.psA.ap()[0:128, 0:T]))(),
                reads=["psA"], writes=[dn], tag="l0bc2_ev")
        for c in range(DCH):
            P.dma("sync", xcld.ap(), sp_xc[:, c * T:(c + 1) * T],
                  reads=[sp_xc.tensor.name], writes=[xcld_n])
            P.dma("sync", dtld.ap(), sp_dt[:, c * T:(c + 1) * T],
                  reads=[sp_dt.tensor.name], writes=[dtld_n])
            P.op("scalar", (lambda c=c, n=n: lambda s: s.activation(
                kb.nm_bc_t.ap(), dtld.ap(), AF.Exp,
                scale=Acol.ap()[:, c * 16 + n:c * 16 + n + 1]))(),
                reads=[dtld_n, "l0_A"], writes=["nm_bc"], tag="l0a_exp")
            u_ap = kb.rs_bc_t.ap()[:, 0:1024].bitcast(BF16)
            P.op("vector", (lambda: lambda v: v.tensor_mul(
                u_ap, dtld.ap(), xcld.ap()))(),
                reads=[dtld_n, xcld_n], writes=[("rs_bc", 0, 1024)], tag="l0u")
            b_ap = kb.rs_bc_t.ap()[:, 1024:2048].bitcast(BF16)
            P.op("vector", (lambda: lambda v: v.tensor_mul(b_ap, u_ap, Bnbc.ap()))(),
                reads=[("rs_bc", 0, 1024), "at_P"], writes=[("rs_bc", 1024, 2048)],
                tag="l0b")
            h_ap = kb.arf2.ap()[:, 2048:3072].bitcast(BF16)
            P.op("vector", (lambda: lambda v: v.tensor_tensor_scan(
                h_ap, kb.nm_bc_t.ap(), b_ap, 0.0, ALU.mult, ALU.add))(),
                reads=["nm_bc", ("rs_bc", 1024, 2048)],
                writes=[("arf2", 2048, 3072)], tag="l0scan")
            tmp_ap = kb.arf2.ap()[:, 3072:4096].bitcast(BF16)
            P.op("vector", (lambda: lambda v: v.tensor_mul(
                tmp_ap, h_ap, Cnbc.ap()))(),
                reads=[("arf2", 2048, 3072), "at_PT"],
                writes=[("arf2", 3072, 4096)], tag="l0hc")
            P.op("vector", (lambda c=c: lambda v: v.tensor_add(
                xres.ap()[:, c * T:(c + 1) * T], xres.ap()[:, c * T:(c + 1) * T],
                tmp_ap))(),
                reads=[("arf2", 3072, 4096), xres.name], writes=[xres.name],
                tag="l0y_add")

    for e in range(NEX):
        emit_expert(e)
        emit_scan_n(2 * e)
        emit_scan_n(2 * e + 1)



def _rmw_rows(kb, sp, c, row_ap, row_names, buf_ap, buf_n, tag):
    P = kb.P
    P.dma("sync", buf_ap, sp[:, c * T:(c + 1) * T], reads=[sp.tensor.name],
          writes=[buf_n])
    P.op("vector", (lambda: lambda v: v.tensor_add(buf_ap, buf_ap, row_ap))(),
         reads=[buf_n] + list(row_names), writes=[buf_n], tag=tag)
    P.dma("sync", sp[:, c * T:(c + 1) * T], buf_ap, reads=[buf_n],
          writes=[sp.tensor.name])


def emit_l1_rms(kb, srcbf, dst_rbc, dst_rbc_n, tag):
    P = kb.P
    sqbf = kb.atP
    for c in range(DCH):
        P.op("scalar", (lambda c=c: lambda s: s.activation(
            sqbf.ap(), srcbf.ap()[:, c * T:(c + 1) * T], AF.Square))(),
            reads=[srcbf.name], writes=["at_P"], tag=f"{tag}_sq")
        for i in range(4):
            n0 = i * 512
            P.op("tensor", (lambda c=c, n0=n0: lambda t: t.matmul(
                kb.psA.ap()[0:1, n0:n0 + 512], kb.ones_colb.ap(),
                sqbf.ap()[:, n0:n0 + 512],
                start=(c == 0), stop=(c == DCH - 1), skip_group_check=True))(),
                reads=["at_P", "c_onescol"], writes=["psA"], tag=f"{tag}_mm")
    sB = kb.arf2.ap()[0:1, 0:T]
    P.op("vector", lambda v: v.tensor_scalar(
        sB, kb.psA.ap()[0:1, 0:T], 1.0 / D, 1e-5, ALU.mult, ALU.add),
        reads=["psA"], writes=[("arf2", 0, 2048)], tag=f"{tag}_v")
    P.op("scalar", lambda s: s.activation(sB, sB, AF.Sqrt),
         reads=[("arf2", 0, 2048)], writes=[("arf2", 0, 2048)])
    P.op("vector", lambda v: v.reciprocal(sB, sB),
         reads=[("arf2", 0, 2048)], writes=[("arf2", 0, 2048)])
    kb.bcast_row(sB, T, dst_rbc, dst_rbc_n, reads=[("arf2", 0, 2048)])


class _V1:
    def __init__(self, ap):
        self._ap = ap

    def ap(self):
        return self._ap


def emit_l1_brA(kb, xres, xnbf, sp_xn, sp_S):
    P = kb.P
    nc = kb.nc
    NH, DS, DIN = 32, 64, 2048
    sp_z = nc.dram_tensor("sp_z", [128, 16 * T], BF16, kind="Internal").ap()
    sp_gn = nc.dram_tensor("sp_gn", [128, 16 * T], BF16, kind="Internal").ap()
    sp_k2 = nc.dram_tensor("sp_k2", [128, 32 * T], BF16, kind="Internal").ap()

    rowacc = kb.arf2.ap()[:, 2048:4096]
    rowacc_n = ("arf2", 2048, 4096)
    rbuf = kb.nm_bc_t.ap()
    rbuf_n = "nm_bc"

    for c in range(DCH):
        P.dma("gpsimd", xnbf.ap()[:, c * T:(c + 1) * T],
              sp_xn[:, c * T:(c + 1) * T], reads=[sp_xn.tensor.name],
              writes=[xnbf.name])
    emit_l1_rms(kb, xnbf, kb.rs_bc_t.ap(), "rs_bc", "l1r1")
    for c in range(DCH):
        P.op("vector", (lambda c=c: lambda v: v.tensor_mul(
            xnbf.ap()[:, c * T:(c + 1) * T], xnbf.ap()[:, c * T:(c + 1) * T],
            kb.rs_bc_t.ap()))(),
            reads=[xnbf.name, "rs_bc"], writes=[xnbf.name], tag="l1r1_ap")

    def rhs_r1(kc):
        return xnbf.ap()[:, kc * T:(kc + 1) * T], [xnbf.name]

    ztm = kb.atS.ap().bitcast(BF16)
    wzbuf = _V1(kb.arb1.ap()[:, 0:4096])
    for nt in range(4):
        kb.stream_w(kb.d["L1_attn_in_proj_w"], 0, D, nt * 512, 512,
                    wzbuf.ap(), ("arb1", 0, 4096))
        for tc in range(16):
            P_ = [kb.pb0, kb.pb1][tc % 2]
            for kc in range(DCH):
                P.op("tensor", (lambda tc=tc, kc=kc, nt=nt, P_=P_: lambda t: t.matmul(
                    P_.ap()[0:128, 0:512],
                    xnbf.ap()[:, kc * T + tc * 128:kc * T + (tc + 1) * 128],
                    wzbuf.ap()[:, kc * 512:(kc + 1) * 512],
                    start=(kc == 0), stop=(kc == DCH - 1)))(),
                    reads=[xnbf.name, ("arb1", 0, 4096)], writes=[P_.name], tag="l1z_mm")
            P.op("vector", (lambda tc=tc, nt=nt, P_=P_: lambda v: v.tensor_copy(
                ztm[:, (tc % 4) * 512:(tc % 4) * 512 + 512], P_.ap()[0:128, 0:512]))(),
                reads=[P_.name], writes=["at_S"], tag="l1z_ev")
            if tc % 4 == 3:
                P.dma("gpsimd", sp_z.rearrange("p (a b) -> p a b", b=T)[
                    :, tc - 3:tc + 1, nt * 512:nt * 512 + 512],
                    ztm.rearrange("p (a b) -> p a b", b=512)[:, 0:4, :],
                    reads=["at_S"], writes=[sp_z.tensor.name])

    sp_xbc = nc.dram_tensor("sp_xbc", [128, 17 * T], BF16, kind="Internal").ap()
    x17 = kb.sc("l1_x17", [128, T], BF16, 12288)
    cw = kb.sc("l1_cw", [128, 17 * 4], F32, 22400)
    cb = kb.sc("l1_cb", [128, 17], F32, 22688)
    P.dma("sync", cw.ap(), kb.d["L1_convW"][:, :], writes=["l1_cw"])
    P.dma("sync", cb.ap(), kb.d["L1_convB"][:, :], writes=["l1_cb"])
    cacc = kb.atS.ap()[:, 0:T]

    def epi_xbc(mi, m0, mw, n0, nw, ps, psn):
        c = m0 // T if False else mi
        P.op("vector", (lambda n0=n0, nw=nw, ps=ps: lambda v: v.tensor_copy(
            x17.ap()[:, n0:n0 + nw], ps))(),
            reads=[psn], writes=["l1_x17"], tag="l1xbc_ev")
        if n0 + nw >= T:
            P.op("vector", (lambda c=c: lambda v: v.tensor_scalar(
                cacc, x17.ap(), cw.ap()[:, c * 4 + 3:c * 4 + 4],
                cb.ap()[:, c:c + 1], ALU.mult, ALU.add))(),
                reads=["l1_x17", "l1_cw", "l1_cb"], writes=["at_S"], tag="l1conv")
            for j in range(3):
                sh = 3 - j
                P.op("vector", (lambda c=c, j=j, sh=sh: lambda v:
                     v.scalar_tensor_tensor(
                    cacc[:, sh:T], x17.ap()[:, 0:T - sh],
                    cw.ap()[:, c * 4 + j:c * 4 + j + 1],
                    cacc[:, sh:T], ALU.mult, ALU.add))(),
                    reads=["l1_x17", "l1_cw", "at_S"], writes=["at_S"], tag="l1conv")
            P.op("scalar", (lambda c=c: lambda s: s.activation(
                x17.ap(), cacc, AF.Silu))(),
                reads=["at_S"], writes=["l1_x17"], tag="l1csilu")
            P.dma("gpsimd",
                  sp_xbc.rearrange("p (tc c f) -> p tc c f", tc=16, c=17)[:, :, c, :],
                  x17.ap().rearrange("p (tc f) -> p tc f", tc=16),
                  reads=["l1_x17"], writes=[sp_xbc.tensor.name])
    linear_fm(kb, "ipj", kb.d["L1_attn_in_proj_w"], D, 2176, rhs_r1, T, epi_xbc,
              tag="l1xbc", m0_start=2048)
    dtb32 = kb.sc("l1_dtb32", [32, 1], F32, 22784)
    P.dma("sync", dtb32.ap(), kb.d["L1_attn_dt_bias"][:, :], writes=["l1_dtb32"])
    wdt_ap, wdt_n = kb.wslot(0)
    kb.stream_w(kb.d["L1_attn_in_proj_w"], 0, D, 4224, 32, wdt_ap, wdt_n)
    l1dt = kb.arf2.ap()[0:32, 0:T]
    l1dt_n = ("arf2", 0, 2048)
    for nt in range(4):
        for kc in range(DCH):
            P.op("tensor", (lambda nt=nt, kc=kc: lambda t: t.matmul(
                kb.pbX.ap()[0:32, 0:512], wdt_ap[:, kc * 32:(kc + 1) * 32],
                xnbf.ap()[:, kc * T + nt * 512:kc * T + nt * 512 + 512],
                start=(kc == 0), stop=(kc == DCH - 1)))(),
                reads=[wdt_n, xnbf.name], writes=["pbX"], tag="l1dt_mm")
        P.op("vector", (lambda nt=nt: lambda v: v.tensor_scalar(
            kb.atS.ap()[0:32, 0:512], kb.pbX.ap()[0:32, 0:512],
            dtb32.ap(), 0.0, ALU.add, ALU.min))(),
            reads=["pbX", "l1_dtb32"], writes=["at_S"], tag="l1sp")
        P.op("scalar", (lambda: lambda s: s.activation(
            kb.atS.ap()[0:32, 0:512], kb.atS.ap()[0:32, 0:512], AF.Exp))(),
            reads=["at_S"], writes=["at_S"])
        P.op("scalar", (lambda: lambda s: s.activation(
            kb.atS.ap()[0:32, 0:512], kb.atS.ap()[0:32, 0:512], AF.Ln, bias=1.0))(),
            reads=["at_S"], writes=["at_S"])
        P.op("vector", (lambda nt=nt: lambda v: v.tensor_scalar(
            kb.atS.ap()[0:32, 512:1024], kb.pbX.ap()[0:32, 0:512],
            dtb32.ap(), 0.0, ALU.add, ALU.max))(),
            reads=["pbX", "l1_dtb32"], writes=["at_S"], tag="l1sp2")
        P.op("vector", (lambda nt=nt: lambda v: v.tensor_add(
            l1dt[:, nt * 512:(nt + 1) * 512], kb.atS.ap()[0:32, 0:512],
            kb.atS.ap()[0:32, 512:1024]))(),
            reads=["at_S"], writes=[l1dt_n], tag="l1sp3")

    A32 = kb.sc("l1_A32", [32, 1], F32, 22816)
    P.dma("sync", A32.ap(), kb.d["L1_attn_A_log"][:, :], writes=["l1_A32"])
    P.op("scalar", lambda s: s.activation(A32.ap(), A32.ap(), AF.Exp),
         reads=["l1_A32"], writes=["l1_A32"])
    P.op("vector", lambda v: v.tensor_scalar_mul(A32.ap(), A32.ap(), -1.0),
         reads=["l1_A32"], writes=["l1_A32"])
    Db = kb.sc("l1_Db", [128, 32], F32, 22848)
    dbr, dbr_n = kb.row_const("L1_attn_Dr", 32)
    kb.bcast_row(dbr, 32, Db.ap(), "l1_Db", reads=[dbr_n])
    mnw = _V1(kb.rs_bc_t.ap()[:, 0:1024].bitcast(BF16))
    mwr, mwr_n = kb.row_const("L1_attn_mnorm_w", 2048)
    kb.bcast_row(mwr, 2048, mnw.ap(), "rs_bc", reads=[mwr_n])
    st = _V1(kb.atS.ap()[0:64, 0:2048])
    P.op("vector", lambda v: v.memset(st.ap(), 0.0), writes=["at_S"])
    sm = kb.sc("l1_sm", [128, 1024], F32, 0)
    smb = kb.sc("l1_smb", [128, 832], BF16, 20736)
    ytmc = _V1(kb.nm_bc_t.ap()[:, 0:1024].bitcast(BF16))
    gnf = kb.sc("l1_gnf", [128, 16 * 128], BF16, 4096)

    zld2 = kb.sc("l1_zld", [128, T], BF16, 8192)

    xst = kb.sc("l1_xst", [128, 17 * 128], BF16, 16384)
    for cch in range(16):
        t0 = cch * 128
        P.dma("sync", xst.ap(),
              sp_xbc.rearrange("p (tc cf) -> p tc cf", tc=16)[:, cch, :],
              reads=[sp_xbc.tensor.name], writes=["l1_xst"])
        P.op("vector", (lambda cch=cch: lambda v: v.tensor_scalar_mul(
            sm.ap()[0:32, 0:128],
            kb.arf2.ap()[0:32, cch * 128:(cch + 1) * 128], A32.ap()))(),
            reads=[l1dt_n, "l1_A32"], writes=["l1_sm"], tag="l1a")
        P.op("vector", lambda v: v.memset(sm.ap()[0:32, 200:201], 0.0), writes=["l1_sm"])
        _z01 = sm.ap()[0:32, 200:201]
        P.op("vector", (lambda _z01=_z01: lambda v: v.tensor_tensor_scan(
            sm.ap()[0:32, 0:128], sm.ap()[0:32, 0:128],
            bAP(_z01, [_z01.ap[0], [0, 128]]), 0.0,
            ALU.add, ALU.add))(),
            reads=["l1_sm"], writes=["l1_sm"], tag="l1acum")
        P.op("vector", lambda v: v.tensor_scalar(
            sm.ap()[0:32, 128:256][:, 0:128], sm.ap()[0:32, 0:128],
            sm.ap()[0:32, 127:128], -1.0, ALU.subtract, ALU.mult),
            reads=["l1_sm"], writes=["l1_sm"], tag="l1rcum")
        P.op("scalar", lambda s: s.activation(
            sm.ap()[0:32, 128:256][:, 0:128], sm.ap()[0:32, 128:256][:, 0:128],
            AF.Exp), reads=["l1_sm"], writes=["l1_sm"])
        P.op("scalar", lambda s: s.activation(
            sm.ap()[0:32, 256:257], sm.ap()[0:32, 127:128], AF.Exp),
            reads=["l1_sm"], writes=["l1_sm"])
        for soff, doff in ((0, 300), (128, 332), (-1, 364)):
            if soff == -1:
                srcap = kb.arf2.ap()[0:32, cch * 128:(cch + 1) * 128]
                srcn = l1dt_n
            else:
                srcap = sm.ap()[0:32, soff:soff + 128]
                srcn = "l1_sm"
            P.op("tensor", (lambda srcap=srcap, doff=doff: lambda t: t.matmul(
                kb.pbX.ap()[0:128, 0:32], srcap, kb.identf.ap()[0:32, 0:32],
                start=True, stop=True, is_transpose=True))(),
                reads=[srcn, "c_identf"], writes=["pbX"], tag="l1tp")
            P.op("vector", (lambda doff=doff: lambda v: v.tensor_copy(
                sm.ap()[:, doff:doff + 32], kb.pbX.ap()[0:128, 0:32]))(),
                reads=["pbX"], writes=["l1_sm"], tag="l1tp_ev")
        acumT = sm.ap()[:, 300:332]
        wT = sm.ap()[:, 332:364]
        dtT = sm.ap()[:, 364:396]
        P.op("scalar", lambda s: s.activation(
            sm.ap()[:, 396:428], acumT, AF.Exp), reads=["l1_sm"], writes=["l1_sm"])
        eacumT = sm.ap()[:, 396:428]
        P.op("tensor", lambda t: t.matmul(
            kb.pbX.ap()[0:1, 64:96], sm.ap()[0:32, 256:257],
            kb.identf.ap()[0:32, 0:32], start=True, stop=True, is_transpose=True),
            reads=["l1_sm", "c_identf"], writes=["pbX"], tag="l1gamr")
        P.op("vector", lambda v: v.tensor_copy(sm.ap()[0:1, 428:460],
             kb.pbX.ap()[0:1, 64:96]), reads=["pbX"], writes=["l1_sm"])
        P.op("tensor", lambda t: t.matmul(
            kb.pbX.ap()[0:64, 96:128], kb.ones_row.ap()[:, 0:64],
            sm.ap()[0:1, 428:460], start=True, stop=True),
            reads=["c_ones", "l1_sm"], writes=["pbX"], tag="l1gamb")
        P.op("vector", lambda v: v.tensor_copy(sm.ap()[0:64, 460:492],
             kb.pbX.ap()[0:64, 96:128]), reads=["pbX"], writes=["l1_sm"])
        gamb = sm.ap()[0:64, 460:492]
        P.dma("sync", smb.ap()[0:64, 640:768], xst.ap()[64:128, 16 * 128:17 * 128],
              reads=["l1_xst"], writes=["l1_smb"])
        P.op("tensor", (lambda cch=cch: lambda t: t.matmul(
            kb.pbX.ap()[0:128, 128:256], xst.ap()[0:64, 16 * 128:17 * 128],
            smb.ap()[0:64, 640:768], start=True, stop=True))(),
            reads=["l1_xst", "l1_smb"], writes=["pbX"], tag="l1gt")
        P.op("vector", lambda v: v.tensor_copy(
            smb.ap()[:, 0:128], kb.pbX.ap()[0:128, 128:256]),
            reads=["pbX"], writes=["l1_smb"], tag="l1gt_ev")
        P.op("tensor", (lambda cch=cch: lambda t: t.transpose(
            kb.pbT.ap()[0:128, 0:64], xst.ap()[0:64, 16 * 128:17 * 128],
            kb.ident.ap()[0:64, 0:64]))(),
            reads=["l1_xst", "c_ident"], writes=["pbT"], tag="l1btm")
        P.op("vector", lambda v: v.tensor_copy(
            smb.ap()[:, 128:192], kb.pbT.ap()[0:128, 0:64]),
            reads=["pbT"], writes=["l1_smb"], tag="l1btm_ev")
        for h in range(NH):
            hc, hh = h // 2, h % 2
            xs_ap = xst.ap()[hh * 64:hh * 64 + 64, hc * 128:(hc + 1) * 128]
            xs_n = "l1_xst"
            idsl = (kb.ident.ap()[0:64, 0:64] if hh == 0
                    else kb.ident.ap()[64:128, 64:128])
            P.op("tensor", (lambda xs_ap=xs_ap, idsl=idsl: lambda t: t.transpose(
                kb.pbT.ap()[0:128, 64:128], xs_ap, idsl))(),
                reads=[xs_n, "c_ident"], writes=["pbT"], tag="l1xtm")
            P.op("vector", lambda v: v.tensor_copy(
                smb.ap()[:, 192:256], kb.pbT.ap()[0:128, 64:128]),
                reads=["pbT"], writes=["l1_smb"], tag="l1xtm_ev")
            P.op("vector", (lambda h=h: lambda v: v.tensor_scalar_mul(
                smb.ap()[:, 256:320], smb.ap()[:, 192:256], dtT[:, h:h + 1]))(),
                reads=["l1_sm", "l1_smb"], writes=["l1_smb"], tag="l1xd")
            P.op("tensor", (lambda h=h: lambda t: t.matmul(
                kb.pbX.ap()[0:1, 0:128], acumT[:, h:h + 1], kb.identf.ap(),
                start=True, stop=True, is_transpose=True))(),
                reads=["l1_sm", "c_identf"], writes=["pbX"], tag="l1ar")
            P.op("vector", lambda v: v.tensor_copy(
                sm.ap()[0:1, 640:768], kb.pbX.ap()[0:1, 0:128]),
                reads=["pbX"], writes=["l1_sm"])
            P.op("vector", lambda v: v.tensor_scalar_mul(
                sm.ap()[0:1, 768:896], sm.ap()[0:1, 640:768], -1.0),
                reads=["l1_sm"], writes=["l1_sm"])
            P.op("tensor", (lambda h=h: lambda t: t.matmul(
                kb.psA.ap()[0:128, 0:128], sm.ap()[0:1, 768:896],
                kb.ones_row.ap()[:, 0:128], start=True, stop=False,
                skip_group_check=True))(),
                reads=["l1_sm", "c_ones"], writes=["psA"], tag="l1d1")
            P.op("tensor", (lambda h=h: lambda t: t.matmul(
                kb.psA.ap()[0:128, 0:128], kb.ones_row.ap()[:, 0:128],
                sm.ap()[0:1, 640:768],
                start=False, stop=True, skip_group_check=True))(),
                reads=["l1_sm", "c_ones"], writes=["psA"], tag="l1d2")
            P.op("vector", (lambda: lambda v: v.tensor_add(
                sm.ap()[:, 512:640], kb.psA.ap()[0:128, 0:128],
                kb.cmaskT.ap()))(),
                reads=["psA", "c_cmaskT"], writes=["l1_sm"], tag="l1msk")
            P.op("scalar", (lambda: lambda s: s.activation(
                smb.ap()[:, 320:448], sm.ap()[:, 512:640], AF.Exp))(),
                reads=["l1_sm"], writes=["l1_smb"], tag="l1lt")
            P.op("vector", (lambda: lambda v: v.tensor_mul(
                smb.ap()[:, 320:448], smb.ap()[:, 320:448], smb.ap()[:, 0:128]))(),
                reads=["l1_smb"], writes=["l1_smb"], tag="l1mt")
            P.op("tensor", (lambda: lambda t: t.matmul(
                kb.pb0.ap()[0:128, 0:64], smb.ap()[:, 320:448],
                smb.ap()[:, 256:320], start=True, stop=True))(),
                reads=["l1_smb"], writes=["pb0"], tag="l1y1")
            P.op("vector", (lambda h=h: lambda v: v.tensor_copy(
                smb.ap()[0:64, 768:832], st.ap()[:, h * 64:(h + 1) * 64]))(),
                reads=["at_S"], writes=["l1_smb"], tag="l1stb")
            P.op("tensor", (lambda cch=cch, h=h: lambda t: t.matmul(
                kb.pb1.ap()[0:128, 0:64], smb.ap()[0:64, 640:768],
                smb.ap()[0:64, 768:832], start=True, stop=True))(),
                reads=["l1_smb"], writes=["pb1"], tag="l1y2")
            P.op("vector", (lambda h=h: lambda v: v.scalar_tensor_tensor(
                smb.ap()[:, 448:512], smb.ap()[:, 192:256], Db.ap()[:, h:h + 1],
                kb.pb0.ap()[0:128, 0:64], ALU.mult, ALU.add))(),
                reads=["l1_smb", "l1_Db", "pb0"], writes=["l1_smb"], tag="l1ev1")
            P.op("vector", (lambda h=h: lambda v: v.scalar_tensor_tensor(
                ytmc.ap()[:, h * 64:(h + 1) * 64], kb.pb1.ap()[0:128, 0:64],
                eacumT[:, h:h + 1], smb.ap()[:, 448:512], ALU.mult, ALU.add))(),
                reads=["pb1", "l1_sm", "l1_smb"], writes=["nm_bc"], tag="l1yc")
            P.op("vector", (lambda h=h: lambda v: v.tensor_scalar_mul(
                smb.ap()[:, 512:576], smb.ap()[:, 128:192], wT[:, h:h + 1]))(),
                reads=["l1_smb", "l1_sm"], writes=["l1_smb"], tag="l1bw")
            P.op("tensor", (lambda h=h: lambda t: t.matmul(
                kb.pbX.ap()[0:64, 256:320], smb.ap()[:, 512:576],
                smb.ap()[:, 256:320], start=True, stop=True))(),
                reads=["l1_smb"], writes=["pbX"], tag="l1hd")
            P.op("vector", (lambda h=h: lambda v: v.scalar_tensor_tensor(
                st.ap()[:, h * 64:(h + 1) * 64], st.ap()[:, h * 64:(h + 1) * 64],
                gamb[:, h:h + 1], kb.pbX.ap()[0:64, 256:320],
                ALU.mult, ALU.add))(),
                reads=["at_S", "l1_sm", "pbX"], writes=["at_S"], tag="l1st")
        P.dma("gpsimd", zld2.ap(),
              sp_z.rearrange("p (a b) -> p a b", b=T)[:, cch, :],
              reads=[sp_z.tensor.name], writes=["l1_zld"])
        P.op("scalar", lambda s: s.activation(zld2.ap(), zld2.ap(), AF.Silu),
             reads=["l1_zld"], writes=["l1_zld"], tag="l1zs")
        P.op("vector", lambda v: v.tensor_mul(ytmc.ap(), ytmc.ap(), zld2.ap()),
             reads=["nm_bc", "l1_zld"], writes=["nm_bc"], tag="l1g")
        P.op("scalar", lambda s: s.activation(
            zld2.ap(), ytmc.ap(), AF.Square, accum_out=sm.ap()[:, 620:621]),
            reads=["nm_bc"], writes=["l1_zld", "l1_sm"], tag="l1gss")
        P.op("vector", lambda v: v.tensor_scalar(
            sm.ap()[:, 621:622], sm.ap()[:, 620:621], 1.0 / DIN, 1e-5,
            ALU.mult, ALU.add), reads=["l1_sm"], writes=["l1_sm"])
        P.op("scalar", lambda s: s.activation(
            sm.ap()[:, 621:622], sm.ap()[:, 621:622], AF.Sqrt),
            reads=["l1_sm"], writes=["l1_sm"])
        P.op("vector", lambda v: v.reciprocal(
            sm.ap()[:, 621:622], sm.ap()[:, 621:622]),
            reads=["l1_sm"], writes=["l1_sm"])
        P.op("vector", lambda v: v.tensor_scalar_mul(
            ytmc.ap(), ytmc.ap(), sm.ap()[:, 621:622]),
            reads=["nm_bc", "l1_sm"], writes=["nm_bc"], tag="l1gn1")
        P.op("vector", lambda v: v.tensor_mul(ytmc.ap(), ytmc.ap(), mnw.ap()),
             reads=["nm_bc", "rs_bc"], writes=["nm_bc"], tag="l1gn2")
        for g0 in range(0, 16, 8):
            for bi in range(g0, g0 + 8):
                kb.transpose_b(ytmc.ap()[:, bi * 128:(bi + 1) * 128], bi - g0,
                               reads=["nm_bc"])
            P.op("vector", (lambda g0=g0: lambda v: v.tensor_copy(
                gnf.ap()[:, g0 * 128:(g0 + 8) * 128], kb.pbT.ap()))(),
                reads=["pbT"], writes=["l1_gnf"], tag="l1gnT")
        P.dma("gpsimd", sp_gn.rearrange("p (a b) -> p a b", b=T)[:, cch, :],
              gnf.ap(), reads=["l1_gnf"], writes=[sp_gn.tensor.name])

    gld = gnf
    for m in range(8):
        wo_ap = kb.warena.ap()[:, 0:2048]
        wo_n = ("warena", 0, 2048)
        kb.stream_w(kb.d["L1_attn_out_proj_w"], 0, DIN, m * 128, 128, wo_ap, wo_n)
        for tc in range(16):
            P.dma("sync", gld.ap(), sp_gn.rearrange("p (a b) -> p a b", b=T)[:, tc, :],
                  reads=[sp_gn.tensor.name], writes=["l1_gnf"])
            for kc in range(16):
                P.op("tensor", (lambda m=m, tc=tc, kc=kc, wo_ap=wo_ap: lambda t:
                     t.matmul(kb.pb0.ap()[0:128, 0:128],
                              wo_ap[:, kc * 128:(kc + 1) * 128],
                              gld.ap()[:, kc * 128:(kc + 1) * 128],
                              start=(kc == 0), stop=(kc == 15)))(),
                    reads=[wo_n, "l1_gnf"], writes=["pb0"], tag="l1op_mm")
            P.op("vector", (lambda tc=tc: lambda v: v.tensor_copy(
                rowacc[:, tc * 128:(tc + 1) * 128], kb.pb0.ap()[0:128, 0:128]))(),
                reads=["pb0"], writes=[rowacc_n], tag="l1op_ev")
        _rmw_rows(kb, sp_S, m, rowacc, [rowacc_n], kb.atS.ap()[:, 0:T], "at_S",
                  "l1op_rmw")


def emit_l1_mqa_cmix(kb, xres, xnbf, sp_S):
    P = kb.P
    nc = kb.nc
    sp_k2 = nc.dram_tensor("sp_k2b", [128, 32 * T], BF16, kind="Internal").ap()
    rowacc = kb.arf2.ap()[:, 2048:4096]
    rowacc_n = ("arf2", 2048, 4096)
    rbuf = kb.nm_bc_t.ap()
    rbuf_n = "nm_bc"
    x17 = kb.sc("l1_x17", [128, T], BF16, 12288)
    zld2 = kb.sc("l1_zld", [128, T], BF16, 8192)

    def cast_load_S(tag):
        for c in range(DCH):
            P.dma("gpsimd", xnbf.ap()[:, c * T:(c + 1) * T],
                  sp_S[:, c * T:(c + 1) * T],
                  reads=[sp_S.tensor.name], writes=[xnbf.name])

    cast_load_S("l1x1")
    emit_l1_rms(kb, xnbf, kb.rs_bc_t.ap(), "rs_bc", "l1r2")
    for c in range(DCH):
        P.op("vector", (lambda c=c: lambda v: v.tensor_mul(
            xnbf.ap()[:, c * T:(c + 1) * T], xnbf.ap()[:, c * T:(c + 1) * T],
            kb.rs_bc_t.ap()))(),
            reads=[xnbf.name, "rs_bc"], writes=[xnbf.name], tag="l1r2_ap")

    def rhs_r2(kc):
        return xnbf.ap()[:, kc * T:(kc + 1) * T], [xnbf.name]

    def epi_q(mi, m0, mw, n0, nw, ps, psn):
        c = m0 // 128
        P.op("vector", (lambda c=c, n0=n0, nw=nw, ps=ps: lambda v: v.tensor_copy(
            kb.arb1.ap()[:, c * T + n0:c * T + n0 + nw], ps))(),
            reads=[psn], writes=[("arb1", c * T, (c + 1) * T)], tag="l1q_ev")
    linear_fm(kb, "mqa_q", kb.d["L1_attn_attn_w"], D, D, rhs_r2, T, epi_q, tag="l1q")

    def epi_k(mi, m0, mw, n0, nw, ps, psn):
        P.op("vector", (lambda n0=n0, nw=nw, ps=ps: lambda v: v.tensor_copy(
            x17.ap()[0:64, n0:n0 + nw], ps))(),
            reads=[psn], writes=["l1_x17"], tag="l1k_ev")
    linear_fm(kb, "mqa_k", kb.d["L1_attn_attn_w"], D, 64, rhs_r2, T, epi_k,
              tag="l1k", m0_start=1024)
    wv_ap, wv_n = kb.wslot(2)
    kb.stream_w(kb.d["L1_attn_attn_w"], 0, D, 1088, 64, wv_ap, wv_n)
    for tc in range(16):
        for kc in range(DCH):
            P.op("tensor", (lambda tc=tc, kc=kc: lambda t: t.matmul(
                kb.pbX.ap()[0:128, 0:64], xnbf.ap()[:, kc * T + tc * 128:kc * T + (tc + 1) * 128],
                wv_ap[:, kc * 64:(kc + 1) * 64],
                start=(kc == 0), stop=(kc == DCH - 1)))(),
                reads=[xnbf.name, wv_n], writes=["pbX"], tag="l1v_mm")
        P.op("vector", (lambda tc=tc: lambda v: v.tensor_copy(
            zld2.ap()[:, tc * 64:(tc + 1) * 64], kb.pbX.ap()[0:128, 0:64]))(),
            reads=["pbX"], writes=["l1_zld"], tag="l1v_ev")
    gnf2 = kb.sc("l1_gnf", [128, 16 * 128], BF16, 4096)
    for h in range(16):
        hc, hh = h // 2, h % 2
        base = hh * 64

        def vv(sc):
            return zld2.ap()[:, sc * 64:(sc + 1) * 64], ["l1_zld"]

        def ow(qt, ps, psn, hc=hc, base=base):
            P.op("vector", (lambda qt=qt, ps=ps, hc=hc, base=base:
                 lambda v: v.tensor_copy(
                xnbf.ap()[base:base + 64, hc * T + qt * 128:hc * T + (qt + 1) * 128],
                ps))(),
                reads=[psn], writes=[xnbf.name], tag="l1mo_ev")

        if hh == 0:
            qap = kb.arb1.ap()[0:64, hc * T:(hc + 1) * T]
            qn = [("arb1", hc * T, (hc + 1) * T)]
        else:
            P.dma("sync", gnf2.ap()[0:64, :],
                  kb.arb1.ap()[64:128, hc * T:(hc + 1) * T],
                  reads=[("arb1", hc * T, (hc + 1) * T)], writes=["l1_gnf"])
            qap = gnf2.ap()[0:64, :]
            qn = ["l1_gnf"]
        attn_head(kb, qap, qn, x17.ap()[0:64, :], ["l1_x17"],
                  vv, T, T, 64, 0.125, True, ow, f"l1h{h}")
    def rhs_o(kc):
        return xnbf.ap()[:, kc * T:(kc + 1) * T], [xnbf.name]
    evst = [None]

    def epi_p(mi, m0, mw, n0, nw, ps, psn):
        P.op("vector", (lambda n0=n0, nw=nw, ps=ps: lambda v: v.tensor_copy(
            rowacc[:, n0:n0 + nw], ps))(),
            reads=[psn], writes=[rowacc_n], tag="l1p_ev")
        if n0 + nw >= T:
            _rmw_rows(kb, sp_S, m0 // 128, rowacc, [rowacc_n], rbuf, rbuf_n, "l1p_rmw")
    linear_fm(kb, "mqa_p", kb.d["L1_attn_proj_w"], D, D, rhs_o, T, epi_p, tag="l1p")

    cast_load_S("l1x2")
    emit_l1_rms(kb, xnbf, kb.rs_bc_t.ap(), "rs_bc", "l1r3")
    for c in range(DCH):
        P.op("vector", (lambda c=c: lambda v: v.tensor_mul(
            xnbf.ap()[:, c * T:(c + 1) * T], xnbf.ap()[:, c * T:(c + 1) * T],
            kb.rs_bc_t.ap()))(),
            reads=[xnbf.name, "rs_bc"], writes=[xnbf.name], tag="l1r3_ap")
    mk = _V1(kb.sc("l1_sm", [128, 1024], F32, 0).ap()[:, 896:928])
    P.dma("sync", mk.ap()[:, 0:8], kb.d["L1_maaKT"][:, :], writes=["l1_sm"])
    P.dma("sync", mk.ap()[:, 8:16], kb.d["L1_maaRT"][:, :], writes=["l1_sm"])
    mk1 = _V1(mk.ap()[:, 16:32])
    P.op("vector", lambda v: v.tensor_scalar(
        mk1.ap(), mk.ap()[:, 0:16], -1.0, 1.0, ALU.mult, ALU.add),
        reads=["l1_sm"], writes=["l1_sm"])

    def build_shift(which):
        off = 0 if which == "k" else 8
        for c in range(DCH):
            P.op("vector", (lambda c=c, off=off: lambda v: v.tensor_scalar_mul(
                kb.arb1.ap()[:, c * T:(c + 1) * T],
                xnbf.ap()[:, c * T:(c + 1) * T], mk1.ap()[:, off + c:off + c + 1]))(),
                reads=[xnbf.name, "l1_sm"], writes=[("arb1", c * T, (c + 1) * T)],
                tag="l1sh1")
            P.op("vector", (lambda c=c, off=off: lambda v: v.scalar_tensor_tensor(
                kb.arb1.ap()[:, c * T + 1:(c + 1) * T],
                xnbf.ap()[:, c * T:(c + 1) * T - 1], mk.ap()[:, off + c:off + c + 1],
                kb.arb1.ap()[:, c * T + 1:(c + 1) * T], ALU.mult, ALU.add))(),
                reads=[xnbf.name, "l1_sm", ("arb1", c * T, (c + 1) * T)],
                writes=[("arb1", c * T, (c + 1) * T)], tag="l1sh2")

    def rhs_sh(kc):
        return kb.arb1.ap()[:, kc * T:(kc + 1) * T], [("arb1", kc * T, (kc + 1) * T)]

    build_shift("k")
    MU = float(np.sqrt(0.5))
    DEN = float(np.sqrt(1.0 / (4.0 * np.pi)) * np.sqrt(2.0))

    SQ2 = float(np.sqrt(2.0))

    def epi_key(mi, m0, mw, n0, nw, ps, psn):
        P.op("vector", (lambda nw=nw, ps=ps: lambda v: v.tensor_scalar(
            kb.atS.ap()[:, 0:nw], ps, SQ2 / DEN, -MU * SQ2 / DEN,
            ALU.mult, ALU.add))(),
            reads=[psn], writes=["at_S"], tag="l1e1")
        P.op("vector", (lambda nw=nw: lambda v: v.tensor_scalar(
            kb.atS.ap()[:, 1024:1024 + nw], kb.atS.ap()[:, 0:nw], 0.0, 2e-4,
            ALU.is_ge, ALU.mult))(),
            reads=["at_S"], writes=["at_S"])
        P.op("vector", (lambda nw=nw: lambda v: v.tensor_scalar(
            kb.atS.ap()[:, 1024:1024 + nw], kb.atS.ap()[:, 1024:1024 + nw],
            -1e-4, 0.0, ALU.add, ALU.add))(),
            reads=["at_S"], writes=["at_S"])
        P.op("vector", (lambda nw=nw: lambda v: v.tensor_add(
            kb.atS.ap()[:, 0:nw], kb.atS.ap()[:, 0:nw],
            kb.atS.ap()[:, 1024:1024 + nw]))(),
            reads=["at_S"], writes=["at_S"], tag="l1e2")
        P.op("scalar", (lambda nw=nw: lambda s: s.activation(
            kb.atS.ap()[:, 512:512 + nw], kb.atS.ap()[:, 0:nw], AF.Gelu))(),
            reads=["at_S"], writes=["at_S"])
        P.op("vector", (lambda nw=nw: lambda v: v.reciprocal(
            kb.atS.ap()[:, 0:nw], kb.atS.ap()[:, 0:nw]))(),
            reads=["at_S"], writes=["at_S"])
        P.op("vector", (lambda n0=n0, nw=nw: lambda v: v.tensor_mul(
            zld2.ap()[:, n0:n0 + nw], kb.atS.ap()[:, 512:512 + nw],
            kb.atS.ap()[:, 0:nw]))(),
            reads=["at_S"], writes=["l1_zld"], tag="l1e3")
        if n0 + nw >= T:
            P.dma("gpsimd", sp_k2.rearrange("p (a b) -> p a b", b=T)[:, m0 // 128, :],
                  zld2.ap(), reads=["l1_zld"], writes=[sp_k2.tensor.name])
    linear_fm(kb, "ckey", kb.d["L1_attn_key_w"], D, 4096, rhs_sh, T, epi_key,
              tag="l1ck")
    build_shift("r")

    def epi_rc(mi, m0, mw, n0, nw, ps, psn):
        c = m0 // 128
        P.op("scalar", (lambda c=c, n0=n0, nw=nw, ps=ps: lambda s: s.activation(
            xnbf.ap()[:, c * T + n0:c * T + n0 + nw], ps, AF.Sigmoid))(),
            reads=[psn], writes=[xnbf.name], tag="l1rc_ev")
    linear_fm(kb, "crec", kb.d["L1_attn_recept_w"], D, D, rhs_sh, T, epi_rc,
              tag="l1cr")
    kvld = x17
    for mh in range(2):
        for nt in range(4):
            n0 = nt * 512
            for kc in range(32):
                wv2_ap, wv2_n = kb.wslot(kc % 2)
                kb.stream_w(kb.d["L1_attn_value_w"], kc * 128, 128,
                            mh * 512, 512, wv2_ap, wv2_n)
                P.dma("sync", kvld.ap()[:, 0:512],
                      sp_k2.rearrange("p (a b) -> p a b", b=T)[:, kc, n0:n0 + 512],
                      reads=[sp_k2.tensor.name], writes=["l1_x17"])
                for m4 in range(4):
                    P.op("tensor", (lambda kc=kc, m4=m4, wv2_ap=wv2_ap: lambda t:
                         t.matmul(kb.psA.ap()[0:128, m4 * 512:(m4 + 1) * 512],
                                  wv2_ap[:, m4 * 128:(m4 + 1) * 128],
                                  kvld.ap()[:, 0:512],
                                  start=(kc == 0), stop=(kc == 31)))(),
                        reads=["l1_x17", wv2_n], writes=["psA"], tag="l1kv_mm")
            for m4 in range(4):
                m = mh * 4 + m4
                P.op("vector", (lambda m=m, m4=m4, n0=n0: lambda v: v.tensor_mul(
                    kb.atS.ap()[:, 0:512],
                    kb.psA.ap()[0:128, m4 * 512:(m4 + 1) * 512],
                    xnbf.ap()[:, m * T + n0:m * T + n0 + 512]))(),
                    reads=["psA", xnbf.name], writes=["at_S"], tag="l1y3")
                P.dma("sync", rbuf[:, 0:512], sp_S[:, m * T + n0:m * T + n0 + 512],
                      reads=[sp_S.tensor.name], writes=[rbuf_n])
                P.op("vector", (lambda: lambda v: v.tensor_add(
                    rbuf[:, 0:512], rbuf[:, 0:512], kb.atS.ap()[:, 0:512]))(),
                    reads=[rbuf_n, "at_S"], writes=[rbuf_n], tag="l1y3a")
                P.dma("sync", sp_S[:, m * T + n0:m * T + n0 + 512], rbuf[:, 0:512],
                      reads=[rbuf_n], writes=[sp_S.tensor.name])


def emit_ln_bf(kb, xnbf, nm_bc, rs_bc, g_col, b_col, gb_reads, tag):
    P = kb.P
    sqbf = kb.atP
    sA = kb.arf2.ap()[0:1, 0:T]
    sTmp = kb.atS.ap()[0:1, 0:T]
    for c in range(DCH):
        for i in range(4):
            n0 = i * 512
            P.op("tensor", (lambda c=c, n0=n0: lambda t: t.matmul(
                kb.psA.ap()[0:1, n0:n0 + 512], kb.ones_colb.ap(),
                xnbf.ap()[:, c * T + n0:c * T + n0 + 512],
                start=(c == 0), stop=(c == DCH - 1)))(),
                reads=[xnbf.name, "c_onescol"], writes=["psA"], tag=f"{tag}_sum")
    P.op("vector", lambda v: v.tensor_scalar_mul(sA, kb.psA.ap()[0:1, 0:T], -1.0 / D),
         reads=["psA"], writes=[("arf2", 0, 2048)], tag=f"{tag}_nm")
    P.op("vector", lambda v: v.tensor_mul(sTmp, sA, sA),
         reads=[("arf2", 0, 2048)], writes=["at_S"])
    kb.bcast_row(sA, T, nm_bc.ap(), nm_bc.name, reads=[("arf2", 0, 2048)])
    for c in range(DCH):
        P.op("scalar", (lambda c=c: lambda s: s.activation(
            sqbf.ap(), xnbf.ap()[:, c * T:(c + 1) * T], AF.Square))(),
            reads=[xnbf.name], writes=["at_P"], tag=f"{tag}_sq")
        for i in range(4):
            n0 = i * 512
            P.op("tensor", (lambda c=c, n0=n0: lambda t: t.matmul(
                kb.psA.ap()[0:1, n0:n0 + 512], kb.ones_colb.ap(),
                sqbf.ap()[:, n0:n0 + 512],
                start=(c == 0), stop=(c == DCH - 1)))(),
                reads=["at_P", "c_onescol"], writes=["psA"], tag=f"{tag}_sumsq")
    P.op("vector", lambda v: v.scalar_tensor_tensor(
        sA, kb.psA.ap()[0:1, 0:T], 1.0 / D, sTmp, ALU.mult, ALU.subtract),
        reads=["psA", "at_S"], writes=[("arf2", 0, 2048)], tag=f"{tag}_var")
    P.op("vector", lambda v: v.tensor_scalar_add(sA, sA, 1e-5),
         reads=[("arf2", 0, 2048)], writes=[("arf2", 0, 2048)])
    P.op("scalar", lambda s: s.activation(sA, sA, AF.Sqrt),
         reads=[("arf2", 0, 2048)], writes=[("arf2", 0, 2048)])
    P.op("vector", lambda v: v.reciprocal(sA, sA),
         reads=[("arf2", 0, 2048)], writes=[("arf2", 0, 2048)])
    kb.bcast_row(sA, T, rs_bc.ap(), rs_bc.name, reads=[("arf2", 0, 2048)])
    for c in range(DCH):
        P.op("vector", (lambda c=c: lambda v: v.tensor_add(
            kb.atS.ap(), xnbf.ap()[:, c * T:(c + 1) * T], nm_bc.ap()))(),
            reads=[xnbf.name, nm_bc.name], writes=["at_S"])
        P.op("vector", (lambda c=c: lambda v: v.tensor_mul(
            kb.atS.ap(), kb.atS.ap(), rs_bc.ap()))(),
            reads=["at_S", rs_bc.name], writes=["at_S"])
        P.op("vector", (lambda c=c: lambda v: v.tensor_scalar(
            xnbf.ap()[:, c * T:(c + 1) * T], kb.atS.ap(),
            g_col(c), b_col(c), ALU.mult, ALU.add))(),
            reads=["at_S"] + list(gb_reads), writes=[xnbf.name], tag=f"{tag}_xn{c}")


def emit_l1_brB(kb, xres, xnbf, sp_xn):
    P = kb.P
    nc = kb.nc
    nm_bc, rs_bc = kb.nm_bc_t, kb.rs_bc_t
    sp_F = nc.dram_tensor("sp_l1F", [128, DCH * T], F32, kind="Internal").ap()
    sp_qkv = nc.dram_tensor("sp_l1qkv", [128, 24 * T], BF16, kind="Internal").ap()
    spF_n, spq_n, spxn_n = sp_F.tensor.name, sp_qkv.tensor.name, sp_xn.tensor.name
    sel2 = kb.sc("bb_sel2", [128, 2], F32, 22976)
    sel2T = kb.sc("bb_sel2T", [2, 128], F32, 23008)
    rotp = kb.sc("bb_rot", [128, 128], BF16, 23520)
    qsks = kb.sc("bb_qs", [128, 12], F32, 23776)
    lnc = kb.sc("bb_ln", [128, 24], F32, 23840)
    b1c = kb.sc("bb_b1", [128, 44], F32, 23936)
    TABOFF = {"cq": 8192, "sq": 10240, "ck": 12288, "sk": 14336}

    def tab_ap(nm):
        return kb.arb1.ap()[:, TABOFF[nm]:TABOFF[nm] + T]

    def tab_n(nm):
        return ("arb1", TABOFF[nm], TABOFF[nm] + T)
    P.dma("sync", sel2.ap(), kb.d["C_sel2"][:, :], writes=["bb_sel2"])
    P.dma("sync", sel2T.ap(), kb.d["C_sel2T"][:, :], writes=["bb_sel2T"])
    P.dma("gpsimd", rotp.ap(), kb.d["C_l1rotP"][:, :], writes=["bb_rot"])
    P.dma("sync", qsks.ap()[:, 0:6], kb.d["L1_qscol"][:, :], writes=["bb_qs"])
    P.dma("sync", qsks.ap()[:, 6:12], kb.d["L1_kscol"][:, :], writes=["bb_qs"])
    for i, nm in enumerate(("C_lm0", "C_lm1", "C_lm0w", "C_lm1w")):
        P.dma("sync", kb.arf2.ap()[:, 2048 + i * 512:2048 + (i + 1) * 512],
              kb.d[nm][:, :], writes=[("arf2", 2048, 4096)])

    def load_shift(src, src_n, tag):
        for c in range(DCH):
            if c < 4:
                P.dma("gpsimd", xnbf.ap()[:, c * T:(c + 1) * T],
                      src[:, c * T:(c + 1) * T], reads=[src_n], writes=[xnbf.name])
            else:
                P.dma("gpsimd", xnbf.ap()[:, c * T + 1:(c + 1) * T],
                      src[:, c * T:(c + 1) * T - 1], reads=[src_n], writes=[xnbf.name])
        for c in range(4, DCH):
            P.op("vector", (lambda c=c: lambda v: v.tensor_scalar_mul(
                xnbf.ap()[:, c * T:c * T + 1], xnbf.ap()[:, c * T + 1:c * T + 2],
                0.0))(),
                reads=[xnbf.name], writes=[xnbf.name], tag=f"{tag}_z{c}")

    def rhs(kc):
        return xnbf.ap()[:, kc * T:(kc + 1) * T], [xnbf.name]

    vstage = kb.atS.ap().bitcast(BF16)
    import os as _os
    for it in range(int(_os.environ.get("MK_BB_IT", "6"))):
        src = sp_xn if it == 0 else sp_F
        src_n = spxn_n if it == 0 else spF_n
        load_shift(src, src_n, f"bb{it}l1")

        def epi_qkv(mi, m0, mw, n0, nw, ps, psn):
            sl = (mi * 4 + n0 // 512) % 4
            P.op("vector", (lambda sl=sl, nw=nw, ps=ps: lambda v: v.tensor_copy(
                kb.atP.ap()[:, sl * 512:sl * 512 + nw], ps))(),
                reads=[psn], writes=[("at_P", sl * 512, sl * 512 + 512)],
                tag=f"bb{it}q_ev")
            P.dma("sync", sp_qkv[:, mi * T + n0:mi * T + n0 + nw],
                  kb.atP.ap()[:, sl * 512:sl * 512 + nw],
                  reads=[("at_P", sl * 512, sl * 512 + 512)], writes=[spq_n])
        linear_fm(kb, f"bbq{it}", kb.d["L1_ffn_Wqkv"][it], D, 3072, rhs, T,
                  epi_qkv, tag=f"bbq{it}")
        for c in range(8):
            P.dma("sync", kb.arb1.ap()[:, 0:T], sp_qkv[:, c * T:(c + 1) * T],
                  reads=[spq_n], writes=[("arb1", 0, 2048)])
            P.dma("sync", kb.arb1.ap()[:, 2048:2048 + T],
                  sp_qkv[:, (8 + c) * T:(9 + c) * T],
                  reads=[spq_n], writes=[("arb1", 2048, 4096)])
            P.dma("sync", vstage[:, 2048:2048 + T],
                  sp_qkv[:, (16 + c) * T:(17 + c) * T],
                  reads=[spq_n], writes=["at_S"])
            for nm in ("cq", "sq", "ck", "sk"):
                P.dma("gpsimd", tab_ap(nm), kb.d[f"C_l1{nm}F"][:, :],
                      writes=[tab_n(nm)])
            for g0 in range(0, 16, 8):
                for sc in range(g0, g0 + 8):
                    kb.transpose_b(vstage[:, 2048 + sc * 128:2048 + (sc + 1) * 128],
                                   sc - g0, reads=["at_S"])
                P.op("vector", (lambda g0=g0: lambda v: v.tensor_copy(
                    kb.arb1.ap()[:, 4096 + g0 * 128:4096 + (g0 + 8) * 128],
                    kb.pbT.ap()))(),
                    reads=["pbT"], writes=[("arb1", 4096, 6144)], tag=f"bb{it}vt")
            for side, boff, tc, ts_, sci in (("q", 0, "cq", "sq", it),
                                             ("k", 2048, "ck", "sk", 6 + it)):
                bb = ("arb1", boff, boff + 2048)
                bap = kb.arb1.ap()[:, boff:boff + 2048]
                P.op("vector", (lambda bap=bap: lambda v: v.tensor_mul(
                    nm_bc.ap(), bap, bap))(),
                    reads=[bb], writes=["nm_bc"], tag=f"bb{it}{side}sq")
                for i in range(4):
                    P.op("tensor", (lambda i=i: lambda t: t.matmul(
                        kb.psA.ap()[0:2, i * 512:(i + 1) * 512], sel2.ap(),
                        nm_bc.ap()[:, i * 512:(i + 1) * 512],
                        start=True, stop=True))(),
                        reads=["nm_bc", "bb_sel2"], writes=["psA"], tag=f"bb{it}{side}ss")
                rrow = kb.arf2.ap()[0:2, 0:T]
                P.op("vector", lambda v: v.tensor_scalar_max(
                    rrow, kb.psA.ap()[0:2, 0:T], 1e-24),
                    reads=["psA"], writes=[("arf2", 0, 2048)])
                P.op("scalar", lambda s: s.activation(rrow, rrow, AF.Sqrt),
                     reads=[("arf2", 0, 2048)], writes=[("arf2", 0, 2048)])
                P.op("vector", lambda v: v.reciprocal(rrow, rrow),
                     reads=[("arf2", 0, 2048)], writes=[("arf2", 0, 2048)])
                for i in range(4):
                    P.op("tensor", (lambda i=i: lambda t: t.matmul(
                        kb.pb0.ap()[0:128, 0:512], sel2T.ap(),
                        kb.arf2.ap()[0:2, i * 512:(i + 1) * 512],
                        start=True, stop=True))(),
                        reads=[("arf2", 0, 2048), "bb_sel2T"], writes=["pb0"],
                        tag=f"bb{it}{side}rb")
                    P.op("vector", (lambda i=i, sci=sci, bap=bap: lambda v: v.scalar_tensor_tensor(
                        bap[:, i * 512:(i + 1) * 512], bap[:, i * 512:(i + 1) * 512],
                        qsks.ap()[:, sci:sci + 1], kb.pb0.ap(),
                        ALU.mult, ALU.mult))(),
                        reads=[bb, "bb_qs", "pb0"], writes=[bb], tag=f"bb{it}{side}nr")
                for i in range(4):
                    P.op("tensor", (lambda i=i, bap=bap: lambda t: t.matmul(
                        kb.pb1.ap()[0:128, 0:512], rotp.ap(),
                        bap[:, i * 512:(i + 1) * 512], start=True, stop=True))(),
                        reads=[bb, "bb_rot"], writes=["pb1"], tag=f"bb{it}{side}rot")
                    P.op("vector", (lambda i=i, bap=bap, tc=tc: lambda v: v.tensor_mul(
                        kb.atS.ap()[:, 0:512], bap[:, i * 512:(i + 1) * 512],
                        tab_ap(tc)[:, i * 512:(i + 1) * 512]))(),
                        reads=[bb, tab_n(tc)], writes=["at_S"])
                    P.op("vector", (lambda i=i, ts_=ts_: lambda v: v.tensor_mul(
                        kb.atS.ap()[:, 512:1024], kb.pb1.ap(),
                        tab_ap(ts_)[:, i * 512:(i + 1) * 512]))(),
                        reads=["pb1", tab_n(ts_)], writes=["at_S"])
                    P.op("vector", (lambda i=i, bap=bap: lambda v: v.tensor_add(
                        bap[:, i * 512:(i + 1) * 512], kb.atS.ap()[:, 0:512],
                        kb.atS.ap()[:, 512:1024]))(),
                        reads=["at_S"], writes=[bb], tag=f"bb{it}{side}rp")
            for hh in range(2):
                base = hh * 64
                for wi in range(8):
                    Tk = 256 if wi == 0 else 512
                    koff = 0 if wi == 0 else (wi - 1) * 256

                    def vvf(sc, wi=wi, base=base):
                        g = sc if wi == 0 else (wi - 1) * 2 + sc
                        return (kb.arb1.ap()[:, 4096 + g * 128 + base:
                                             4096 + g * 128 + base + 64],
                                [("arb1", 4096, 6144)])

                    def msk(qt, Sf, Lk, wi=wi):
                        if wi == 0:
                            c0 = 3072 + qt * 512 + 256
                        else:
                            c0 = 2048 + qt * 512
                        P.op("vector", (lambda qt=qt, Lk=Lk, c0=c0, Sf=Sf:
                             lambda v: v.tensor_add(
                                 Sf.ap()[:, 0:Lk], Sf.ap()[:, 0:Lk],
                                 kb.arf2.ap()[:, c0:c0 + Lk]))(),
                            reads=["at_S", ("arf2", 2048, 4096)], writes=["at_S"])

                    def oww(qt, ps, psn, wi=wi, hh=hh, c=c):
                        P.op("vector", (lambda qt=qt, ps=ps, wi=wi, hh=hh, c=c:
                             lambda v: v.tensor_copy(
                                 xnbf.ap()[hh * 64:hh * 64 + 64,
                                           c * T + wi * 256 + qt * 128:
                                           c * T + wi * 256 + (qt + 1) * 128], ps))(),
                            reads=[psn], writes=[xnbf.name], tag=f"bb{it}ao")

                    attn_head(kb,
                              kb.arb1.ap()[base:base + 64, wi * 256:wi * 256 + 256],
                              [("arb1", 0, 2048)],
                              kb.arb1.ap()[base:base + 64, 2048 + koff:2048 + koff + Tk],
                              [("arb1", 2048, 4096)],
                              vvf, 256, Tk, 64, 8.0, False, oww,
                              f"bb{it}h{c}{hh}w{wi}", extra_mask=msk)
        def epi_wo(mi, m0, mw, n0, nw, ps, psn):
            sl = (mi * 4 + n0 // 512) % 4
            rsl = ("rs_bc", sl * 512, sl * 512 + 512)
            P.dma("sync", rs_bc.ap()[:, sl * 512:sl * 512 + nw],
                  src[:, mi * T + n0:mi * T + n0 + nw], reads=[src_n], writes=[rsl])
            P.op("vector", (lambda sl=sl, nw=nw, ps=ps: lambda v: v.tensor_add(
                rs_bc.ap()[:, sl * 512:sl * 512 + nw], ps,
                rs_bc.ap()[:, sl * 512:sl * 512 + nw]))(),
                reads=[psn, rsl], writes=[rsl], tag=f"bb{it}o_rmw")
            P.dma("sync", sp_F[:, mi * T + n0:mi * T + n0 + nw],
                  rs_bc.ap()[:, sl * 512:sl * 512 + nw], reads=[rsl], writes=[spF_n])
        linear_fm(kb, f"bbo{it}", kb.d["L1_ffn_Wo"][it], D, D, rhs, T, epi_wo,
                  tag=f"bbo{it}")
        if _os.environ.get("MK_BB_STOP") == "attn":
            break
        load_shift(sp_F, spF_n, f"bb{it}l2")
        P.dma("sync", lnc.ap()[:, 0:8], kb.d["L1_lngT"][it], writes=["bb_ln"])
        P.dma("sync", lnc.ap()[:, 8:16], kb.d["L1_lnbT"][it], writes=["bb_ln"])
        P.dma("sync", lnc.ap()[:, 16:24], kb.d["L1_b2T"][it], writes=["bb_ln"])
        P.dma("sync", b1c.ap(), kb.d["L1_b1T"][it], writes=["bb_b1"])
        emit_ln_bf(kb, xnbf, nm_bc, rs_bc,
                   lambda c: lnc.ap()[:, c:c + 1], lambda c: lnc.ap()[:, 8 + c:9 + c],
                   ["bb_ln"], f"bbl{it}")
        w1 = kb.d["L1_W1p"][it]
        w2 = kb.d["L1_W2p"][it]
        w2sl = ("arb1", 11264, 14080)
        for tq in range(4):
            n0g = tq * 512
            for mb in range(22):
                wg_ap, wg_n = kb.wslot(0)
                wa_ap, wa_n = kb.wslot(1)
                kb.stream_w(w1, 0, D, (22 + mb) * 128, 128, wg_ap, wg_n, queue="sync")
                kb.stream_w(w1, 0, D, mb * 128, 128, wa_ap, wa_n, queue="sync")
                for kc in range(DCH):
                    P.op("tensor", (lambda kc=kc, n0g=n0g, wg_ap=wg_ap: lambda t: t.matmul(
                        kb.pb0.ap()[0:128, 0:512], wg_ap[:, kc * 128:(kc + 1) * 128],
                        xnbf.ap()[:, kc * T + n0g:kc * T + n0g + 512],
                        start=(kc == 0), stop=(kc == DCH - 1)))(),
                        reads=[wg_n, xnbf.name], writes=["pb0"], tag=f"bbf{it}g")
                for kc in range(DCH):
                    P.op("tensor", (lambda kc=kc, n0g=n0g, wa_ap=wa_ap: lambda t: t.matmul(
                        kb.pb1.ap()[0:128, 0:512], wa_ap[:, kc * 128:(kc + 1) * 128],
                        xnbf.ap()[:, kc * T + n0g:kc * T + n0g + 512],
                        start=(kc == 0), stop=(kc == DCH - 1)))(),
                        reads=[wa_n, xnbf.name], writes=["pb1"], tag=f"bbf{it}a")
                P.op("scalar", (lambda mb=mb: lambda s: s.activation(
                    kb.atP.ap()[:, 0:512], kb.pb0.ap(), AF.Gelu,
                    bias=b1c.ap()[:, 22 + mb:23 + mb], scale=1.0))(),
                    reads=["pb0", "bb_b1"], writes=[("at_P", 0, 512)], tag=f"bbf{it}gl")
                P.op("vector", (lambda mb=mb: lambda v: v.scalar_tensor_tensor(
                    kb.arb1.ap()[:, mb * 512:(mb + 1) * 512], kb.pb1.ap(),
                    b1c.ap()[:, mb:mb + 1], kb.atP.ap()[:, 0:512],
                    ALU.add, ALU.mult))(),
                    reads=["pb1", "bb_b1", ("at_P", 0, 512)],
                    writes=[("arb1", mb * 512, (mb + 1) * 512)], tag=f"bbf{it}h")
            for mo in range(8):
                kb.stream_w(w2, 0, 2816, mo * 128, 128,
                            kb.arb1.ap()[:, 11264:14080], w2sl, queue="sync")
                for kc in range(22):
                    P.op("tensor", (lambda kc=kc: lambda t: t.matmul(
                        kb.pb0.ap()[0:128, 0:512],
                        kb.arb1.ap()[:, 11264 + kc * 128:11264 + (kc + 1) * 128],
                        kb.arb1.ap()[:, kc * 512:(kc + 1) * 512],
                        start=(kc == 0), stop=(kc == 21)))(),
                        reads=[w2sl, ("arb1", kc * 512, (kc + 1) * 512)],
                        writes=["pb0"], tag=f"bbf{it}y")
                sl = mo % 4
                rsl = ("rs_bc", sl * 512, sl * 512 + 512)
                P.dma("sync", rs_bc.ap()[:, sl * 512:sl * 512 + 512],
                      sp_F[:, mo * T + n0g:mo * T + n0g + 512],
                      reads=[spF_n], writes=[rsl])
                P.op("vector", (lambda mo=mo, sl=sl: lambda v: v.scalar_tensor_tensor(
                    rs_bc.ap()[:, sl * 512:sl * 512 + 512], kb.pb0.ap(),
                    lnc.ap()[:, 16 + mo:17 + mo],
                    rs_bc.ap()[:, sl * 512:sl * 512 + 512], ALU.add, ALU.add))(),
                    reads=["pb0", "bb_ln", rsl], writes=[rsl], tag=f"bbf{it}r")
                P.dma("sync", sp_F[:, mo * T + n0g:mo * T + n0g + 512],
                      rs_bc.ap()[:, sl * 512:sl * 512 + 512],
                      reads=[rsl], writes=[spF_n])
    for c in range(DCH):
        P.dma("sync", nm_bc.ap(), sp_F[:, c * T:(c + 1) * T],
              reads=[spF_n], writes=["nm_bc"])
        P.op("vector", (lambda c=c: lambda v: v.tensor_add(
            xres.ap()[:, c * T:(c + 1) * T], xres.ap()[:, c * T:(c + 1) * T],
            nm_bc.ap()))(),
            reads=["nm_bc", xres.name], writes=[xres.name], tag="bbF_add")


def emit_l1(kb, xres, xnbf, nm_bc, rs_bc):
    P = kb.P
    nc = kb.nc
    sp_xn = nc.dram_tensor("sp_l1xn", [128, DCH * T], F32, kind="Internal").ap()
    sp_S = nc.dram_tensor("sp_l1S", [128, DCH * T], F32, kind="Internal").ap()
    for c in range(DCH):
        emit_xn32(kb, xres, nm_bc, rs_bc, c, kb.atS.ap(), "at_S")
        P.dma("sync", sp_xn[:, c * T:(c + 1) * T], kb.atS.ap(),
              reads=["at_S"], writes=[sp_xn.tensor.name])
        P.dma("sync", sp_S[:, c * T:(c + 1) * T], kb.atS.ap(),
              reads=["at_S"], writes=[sp_S.tensor.name])
    if MK_PARTS == "all" or "bB" in MK_PARTS.split(","):
        emit_l1_brB(kb, xres, xnbf, sp_xn)
    if MK_PARTS == "all" or "bA" in MK_PARTS.split(","):
        emit_l1_brA(kb, xres, xnbf, sp_xn, sp_S)
        emit_l1_mqa_cmix(kb, xres, xnbf, sp_S)
        for c in range(DCH):
            P.dma("sync", kb.nm_bc_t.ap(), sp_S[:, c * T:(c + 1) * T],
                  reads=[sp_S.tensor.name], writes=["nm_bc"])
            P.op("vector", (lambda c=c: lambda v: v.tensor_add(
                xres.ap()[:, c * T:(c + 1) * T], xres.ap()[:, c * T:(c + 1) * T],
                kb.nm_bc_t.ap()))(),
                reads=["nm_bc", xres.name], writes=[xres.name], tag="l1S_add")

def emit_l2_attn(kb, xres, xnbf):
    P = kb.P
    qb_ap = kb.arf2.ap()[:, 0:1024].bitcast(BF16)
    qb_n = ("arf2", 0, 1024)
    kbf_ap = kb.arf2.ap()[:, 1024:2048].bitcast(BF16)
    kbf_n = ("arf2", 1024, 2048)
    vt_ap = kb.arf2.ap()[:, 2048:3072].bitcast(BF16)
    vt_n = ("arf2", 2048, 3072)
    rr_ap = kb.arf2.ap()[:, 3072:4096].bitcast(BF16)
    rr_n = ("arf2", 3072, 4096)
    outfm = kb.arb1
    bq = kb.sc("l2_bq", [128, 16], F32, 10240)
    bvb = kb.sc("l2_bvb", [128, 1024], BF16, 8192)
    rotp = kb.sc("l2_rotp", [128, 128], BF16, 10304)
    ctab = kb.sc("l2_ctab", [128, T], BF16, 0)
    stab = kb.sc("l2_stab", [128, T], BF16, 4096)
    P.dma("gpsimd", rotp.ap(), kb.d["C_l2rotP"][:, :], writes=["l2_rotp"])
    P.dma("gpsimd", ctab.ap(), kb.d["C_l2cos"][:, :], writes=["l2_ctab"])
    P.dma("gpsimd", stab.ap(), kb.d["C_l2sin"][:, :], writes=["l2_stab"])
    P.dma("sync", bq.ap(), kb.d["L2_attn_b_qkT"][:, :], writes=["l2_bq"])
    bvr, bvr_n = kb.row_const("L2_attn_b_v", 1024)
    kb.bcast_row(bvr, 1024, bvb.ap(), "l2_bvb", reads=[bvr_n])

    def rhs(kc):
        return xnbf.ap()[:, kc * T:(kc + 1) * T], [xnbf.name]

    for hp in range(8):
        for which, dst_ap, dst_n in (("q", qb_ap, qb_n), ("k", kbf_ap, kbf_n)):
            m0s = hp * 128 if which == "q" else 1024 + hp * 128
            bi = (0 if which == "q" else 8) + hp
            bcol = bq.ap()[:, bi:bi + 1]

            def epi(mi, m0, mw, n0, nw, ps, psn, dst_ap=dst_ap, dst_n=dst_n, bcol=bcol):
                P.op("vector", (lambda n0=n0, nw=nw, ps=ps, bcol=bcol:
                     lambda v: v.tensor_scalar_add(rr_ap[:, n0:n0 + nw], ps, bcol))(),
                     reads=[psn, "l2_bq"], writes=[rr_n], tag="l2qk_ev")
            linear_fm(kb, "L2_attn_w_qkv", kb.d["L2_attn_w_qkv"], D, 128, rhs, T, epi,
                      tag="l2qk", m0_start=m0s)
            for ni in range(4):
                n0 = ni * 512
                P.op("tensor", (lambda n0=n0: lambda t: t.matmul(
                    kb.pb0.ap()[0:128, 0:512], rotp.ap(), rr_ap[:, n0:n0 + 512],
                    start=True, stop=True))(),
                    reads=["l2_rotp", rr_n], writes=["pb0"], tag="l2rot_mm")
                P.op("vector", (lambda n0=n0: lambda v: v.tensor_mul(
                    kb.atS.ap()[:, 0:512], rr_ap[:, n0:n0 + 512], ctab.ap()[:, n0:n0 + 512]))(),
                    reads=[rr_n, "l2_ctab"], writes=["at_S"], tag="l2rope_c")
                P.op("vector", (lambda n0=n0: lambda v: v.tensor_mul(
                    kb.atS.ap()[:, 512:1024], kb.pb0.ap()[0:128, 0:512], stab.ap()[:, n0:n0 + 512]))(),
                    reads=["pb0", "l2_stab"], writes=["at_S"], tag="l2rope_s")
                P.op("vector", (lambda n0=n0, dst_ap=dst_ap: lambda v: v.tensor_add(
                    dst_ap[:, n0:n0 + 512], kb.atS.ap()[:, 0:512], kb.atS.ap()[:, 512:1024]))(),
                    reads=["at_S"], writes=[dst_n], tag="l2rope_a")
        wvb_ap, wvb_n = kb.wslot(2)
        kb.stream_w(kb.d["L2_attn_w_qkv"], 0, D, 2048 + hp * 128, 128, wvb_ap, wvb_n)
        for tc in range(16):
            for kc in range(DCH):
                P.op("tensor", (lambda tc=tc, kc=kc, wvb_ap=wvb_ap: lambda t: t.matmul(
                    kb.pb0.ap()[0:128, 0:128],
                    xnbf.ap()[:, kc * T + tc * 128:kc * T + (tc + 1) * 128],
                    wvb_ap[:, kc * 128:(kc + 1) * 128],
                    start=(kc == 0), stop=(kc == DCH - 1)))(),
                    reads=[xnbf.name, wvb_n], writes=["pb0"], tag="l2v_mm")
            P.op("vector", (lambda tc=tc, hp=hp: lambda v: v.tensor_add(
                vt_ap[:, tc * 128:(tc + 1) * 128], kb.pb0.ap()[0:128, 0:128],
                bvb.ap()[:, hp * 128:(hp + 1) * 128]))(),
                reads=["pb0", "l2_bvb"], writes=[vt_n], tag="l2v_ev")
        for hh in range(2):
            h = hp * 2 + hh
            base = hh * 64

            def vv(sc, base=base):
                return vt_ap[:, sc * 128 + base:sc * 128 + base + 64], [vt_n]

            def ow(qt, ps, psn, base=base, hp=hp):
                P.op("vector", (lambda qt=qt, ps=ps, base=base, hp=hp: lambda v: v.tensor_copy(
                    outfm.ap()[base:base + 64, hp * T + qt * 128:hp * T + (qt + 1) * 128], ps))(),
                    reads=[psn], writes=["arb1"], tag="l2o_ev")

            attn_head(kb, qb_ap[base:base + 64, :], [qb_n],
                      kbf_ap[base:base + 64, :], [kbf_n],
                      vv, T, T, 64, 0.125, False, ow, f"l2h{h}")
    bob = kb.sc("l2_bob", [128, 8], F32, 10560)
    P.dma("sync", bob.ap(), kb.d["L2_attn_b_outT"][:, :], writes=["l2_bob"])

    def rhs_o(kc):
        return outfm.ap()[:, kc * T:(kc + 1) * T], ["arb1"]

    def epi_o(mi, m0, mw, n0, nw, ps, psn):
        c = m0 // 128
        P.op("vector", (lambda c=c, n0=n0, nw=nw, ps=ps: lambda v: v.scalar_tensor_tensor(
            xres.ap()[:, c * T + n0:c * T + n0 + nw], ps, bob.ap()[:, c:c + 1],
            xres.ap()[:, c * T + n0:c * T + n0 + nw], ALU.add, ALU.add))(),
            reads=[psn, "l2_bob", xres.name], writes=[xres.name], tag="l2wo_ev")
    linear_fm(kb, "L2_attn_w_out", kb.d["L2_attn_w_out"], D, D, rhs_o, T, epi_o, tag="l2wo")


def emit_l2_ffn(kb, xres, xnbf, nm_bc, rs_bc, stage="all"):
    P = kb.P
    E_, I_ = 16, 768
    if stage in ("all", "router"):
        emit_l2_router(kb, xres, xnbf, nm_bc, rs_bc, comb, comb_n)
    if stage == "router":
        return
    emit_l2_experts(kb, xres, xnbf, comb, comb_n)
    return


def emit_l2_router(kb, xres, xnbf, nm_bc, rs_bc, comb, comb_n):
    P = kb.P
    gw = kb.sc("l2_gw", [128, 128], F32, 10592)
    kb.stream_w32(kb.d["L2_ffn_gate_wT"], 0, D, 0, 16, gw.ap(), "l2_gw")
    gbr, gbr_n = kb.row_const("L2_ffn_gate_bias", 16)
    gbb = kb.sc("l2_gbb", [128, 16], F32, 11104)
    kb.bcast_row(gbr, 16, gbb.ap(), "l2_gbb", reads=[gbr_n])
    comb = kb.nm_bc_t
    comb_n = ("nm_bc", 0, 256)
    RTt = kb.arf2
    class _RT:
        def ap(self):
            return RTt.ap()[:, 256:384]
    RT = _RT()
    xn32 = kb.atS
    for kc in range(DCH):
        emit_xn32(kb, xres, nm_bc, rs_bc, kc, xn32.ap()[:, 0:T], "at_S")
        for tt in range(16):
            P.op("tensor", (lambda tt=tt, kc=kc: lambda t: t.matmul(
                kb.pbX.ap()[0:128, tt * 16:(tt + 1) * 16],
                xn32.ap()[:, tt * 128:(tt + 1) * 128],
                gw.ap()[:, kc * 16:(kc + 1) * 16],
                start=(kc == 0 and tt == 0), stop=(kc == DCH - 1 and tt == 15),
                skip_group_check=True))(),
                reads=["at_S", "l2_gw"], writes=["pbX"], tag="l2r_mm")
    for tt in range(16):
        sc = RT.ap()[:, 0:16]
        sb_ = RT.ap()[:, 16:32]
        mk = RT.ap()[:, 32:48]
        t1 = RT.ap()[:, 48:49]
        eq = RT.ap()[:, 49:53]
        cu = RT.ap()[:, 53:57]
        gsum = RT.ap()[:, 57:61]
        zz = RT.ap()[:, 61:62]
        m2s = RT.ap()[:, 62:63]
        gm = RT.ap()[:, 66:70]
        cur = RT.ap()[:, 70:74]
        am = RT.ap()[:, 74:90]
        eq16 = RT.ap()[:, 90:106]
        cu16 = RT.ap()[:, 106:122]
        P.op("scalar", (lambda tt=tt: lambda s: s.activation(
            sc, kb.pbX.ap()[0:128, tt * 16:(tt + 1) * 16], AF.Sigmoid))(),
            reads=["pbX"], writes=[("arf2", 256, 384)], tag="l2r_sig")
        P.op("vector", lambda v: v.tensor_add(sb_, sc, gbb.ap()),
             reads=[("arf2", 256, 384), "l2_gbb"], writes=[("arf2", 256, 384)])
        P.op("vector", lambda v: v.memset(zz, 0.0), writes=[("arf2", 256, 384)])
        for g in range(4):
            gsl = RT.ap()[:, 16 + g * 4:16 + g * 4 + 4]
            P.op("vector", (lambda gsl=gsl: lambda v: v.tensor_reduce(t1, gsl, AX.X, ALU.max))(),
                 reads=[("arf2", 256, 384)], writes=[("arf2", 256, 384)])
            P.op("vector", (lambda gsl=gsl: lambda v: v.tensor_scalar(
                eq, gsl, t1, None, ALU.is_ge))(),
                 reads=[("arf2", 256, 384)], writes=[("arf2", 256, 384)])
            P.op("vector", (lambda: lambda v: v.tensor_tensor_scan(
                cu, eq, bAP(zz, [zz.ap[0], [0, 4]]), 0.0, ALU.add, ALU.add))(),
                 reads=[("arf2", 256, 384)], writes=[("arf2", 256, 384)])
            P.op("vector", (lambda: lambda v: v.tensor_scalar(
                cu, cu, 1.0, None, ALU.is_le))(), reads=[("arf2", 256, 384)], writes=[("arf2", 256, 384)])
            P.op("vector", (lambda: lambda v: v.tensor_mul(eq, eq, cu))(),
                 reads=[("arf2", 256, 384)], writes=[("arf2", 256, 384)])
            P.op("vector", (lambda gsl=gsl: lambda v: v.scalar_tensor_tensor(
                cu, eq, -1e30, gsl, ALU.mult, ALU.add))(),
                 reads=[("arf2", 256, 384)], writes=[("arf2", 256, 384)])
            P.op("vector", (lambda: lambda v: v.tensor_reduce(m2s, cu, AX.X, ALU.max))(),
                 reads=[("arf2", 256, 384)], writes=[("arf2", 256, 384)])
            P.op("vector", (lambda g=g: lambda v: v.tensor_add(
                gsum[:, g:g + 1], t1, m2s))(),
                 reads=[("arf2", 256, 384)], writes=[("arf2", 256, 384)])
        P.op("vector", lambda v: v.memset(gm, 0.0), writes=[("arf2", 256, 384)])
        P.op("vector", lambda v: v.tensor_copy(cur, gsum), reads=[("arf2", 256, 384)], writes=[("arf2", 256, 384)])
        for it in range(2):
            P.op("vector", lambda v: v.tensor_reduce(t1, cur, AX.X, ALU.max),
                 reads=[("arf2", 256, 384)], writes=[("arf2", 256, 384)])
            P.op("vector", lambda v: v.tensor_scalar(eq, cur, t1, None, ALU.is_ge),
                 reads=[("arf2", 256, 384)], writes=[("arf2", 256, 384)])
            P.op("vector", (lambda: lambda v: v.tensor_tensor_scan(
                cu, eq, bAP(zz, [zz.ap[0], [0, 4]]), 0.0, ALU.add, ALU.add))(),
                 reads=[("arf2", 256, 384)], writes=[("arf2", 256, 384)])
            P.op("vector", lambda v: v.tensor_scalar(cu, cu, 1.0, None, ALU.is_le),
                 reads=[("arf2", 256, 384)], writes=[("arf2", 256, 384)])
            P.op("vector", lambda v: v.tensor_mul(eq, eq, cu),
                 reads=[("arf2", 256, 384)], writes=[("arf2", 256, 384)])
            P.op("vector", lambda v: v.tensor_add(gm, gm, eq),
                 reads=[("arf2", 256, 384)], writes=[("arf2", 256, 384)])
            P.op("vector", lambda v: v.scalar_tensor_tensor(
                cur, eq, -1e30, cur, ALU.mult, ALU.add),
                 reads=[("arf2", 256, 384)], writes=[("arf2", 256, 384)])
        gmrep = bAP(gm, [gm.ap[0], [1, 4], [0, 4]])
        P.op("vector", lambda v: v.tensor_copy(
            mk.rearrange("p (a b) -> p a b", a=4), gmrep),
            reads=[("arf2", 256, 384)], writes=[("arf2", 256, 384)])
        P.op("vector", lambda v: v.tensor_scalar(mk, mk, -1.0, 1e30, ALU.add, ALU.mult),
             reads=[("arf2", 256, 384)], writes=[("arf2", 256, 384)])
        P.op("vector", lambda v: v.tensor_add(mk, mk, sb_),
             reads=[("arf2", 256, 384)], writes=[("arf2", 256, 384)])
        P.op("vector", lambda v: v.memset(am, 0.0), writes=[("arf2", 256, 384)])
        for it in range(4):
            P.op("vector", lambda v: v.tensor_reduce(t1, mk, AX.X, ALU.max),
                 reads=[("arf2", 256, 384)], writes=[("arf2", 256, 384)])
            P.op("vector", lambda v: v.tensor_scalar(eq16, mk, t1, None, ALU.is_ge),
                 reads=[("arf2", 256, 384)], writes=[("arf2", 256, 384)])
            P.op("vector", (lambda: lambda v: v.tensor_tensor_scan(
                cu16, eq16, bAP(zz, [zz.ap[0], [0, 16]]), 0.0, ALU.add, ALU.add))(),
                 reads=[("arf2", 256, 384)], writes=[("arf2", 256, 384)])
            P.op("vector", lambda v: v.tensor_scalar(cu16, cu16, 1.0, None, ALU.is_le),
                 reads=[("arf2", 256, 384)], writes=[("arf2", 256, 384)])
            P.op("vector", lambda v: v.tensor_mul(eq16, eq16, cu16),
                 reads=[("arf2", 256, 384)], writes=[("arf2", 256, 384)])
            P.op("vector", lambda v: v.tensor_add(am, am, eq16),
                 reads=[("arf2", 256, 384)], writes=[("arf2", 256, 384)])
            P.op("vector", lambda v: v.scalar_tensor_tensor(
                mk, eq16, -1e30, mk, ALU.mult, ALU.add),
                 reads=[("arf2", 256, 384)], writes=[("arf2", 256, 384)])
        P.op("vector", lambda v: v.tensor_mul(am, am, sc),
             reads=[("arf2", 256, 384)], writes=[("arf2", 256, 384)])
        P.op("vector", lambda v: v.tensor_reduce(t1, am, AX.X, ALU.add),
             reads=[("arf2", 256, 384)], writes=[("arf2", 256, 384)])
        P.op("vector", lambda v: v.tensor_scalar_add(t1, t1, 1e-20),
             reads=[("arf2", 256, 384)], writes=[("arf2", 256, 384)])
        P.op("vector", lambda v: v.reciprocal(t1, t1),
             reads=[("arf2", 256, 384)], writes=[("arf2", 256, 384)])
        P.op("vector", (lambda tt=tt: lambda v: v.tensor_scalar(
            comb.ap()[:, tt * 16:(tt + 1) * 16], am, t1, 2.5, ALU.mult, ALU.mult))(),
            reads=[("arf2", 256, 384)], writes=[comb_n], tag="l2r_comb")
    return


def emit_l2_experts(kb, xres, xnbf, comb, comb_n):
    P = kb.P
    E_, I_ = 16, 768
    combTb2 = kb.atP
    combTb_ap = kb.arb1.ap()[0:16, 4096:4096 + T]
    combTb_n = ("arb1", 4096, 4096 + T)
    P.op("vector", lambda v: v.tensor_copy(combTb2.ap()[:, 0:256], comb.ap()[:, 0:256]),
         reads=[comb_n], writes=["at_P"])
    for tt in range(16):
        P.op("tensor", (lambda tt=tt: lambda t: t.transpose(
            kb.pbT.ap()[0:16, 0:128],
            combTb2.ap()[:, tt * 16:(tt + 1) * 16], kb.ident.ap()))(),
            reads=["at_P", "c_ident"], writes=["pbT"], tag="l2cT_mm")
        P.op("vector", (lambda tt=tt: lambda v: v.tensor_copy(
            combTb_ap[:, tt * 128:(tt + 1) * 128], kb.pbT.ap()[0:16, 0:128]))(),
            reads=["pbT"], writes=[combTb_n], tag="l2cT_ev")
    selall_ap = kb.arb1.ap()[0:16, 6144:6144 + 2048]
    selall_n = ("arb1", 6144, 6144 + 2048)
    P.dma("gpsimd", selall_ap, kb.d["C_selall"][:, :], writes=[selall_n])
    combBC_ap = kb.arb1.ap()[:, 8192:8704]
    combBC_n = ("arb1", 8192, 8704)
    yacc = kb.arf2
    hs_ap = kb.atP.ap()[:, 512:1024]
    hs_n = ("at_P", 512, 1024)
    hb_ap = kb.arb1.ap()[:, 0:3072]
    hb_n = ("arb1", 0, 3072)
    for nt in range(4):
        n0 = nt * 512
        first = True
        for e in range(17):
            we1 = "L2_ffn_w1T" if e < 16 else "L2_ffn_ws1T"
            we3 = "L2_ffn_w3T" if e < 16 else "L2_ffn_ws3T"
            we2 = "L2_ffn_w2T" if e < 16 else "L2_ffn_ws2T"
            if e < 16:
                P.op("tensor", (lambda e=e, n0=n0: lambda t: t.matmul(
                    kb.pbX.ap()[0:128, 0:512], selall_ap[:, e * 128:e * 128 + 128],
                    combTb_ap[:, n0:n0 + 512], start=True, stop=True))(),
                    reads=[selall_n, combTb_n], writes=["pbX"], tag="l2bc_mm")
                P.op("vector", (lambda: lambda v: v.tensor_copy(
                    combBC_ap, kb.pbX.ap()[0:128, 0:512]))(),
                    reads=["pbX"], writes=[combBC_n], tag="l2bc_ev")
            for mi in range(6):
                w1_ap, w1_n = kb.wslot(0)
                w3_ap, w3_n = kb.wslot(1)
                kb.stream_w(kb.d[we1] if e == 16 else kb.d[we1][e], 0, D, mi * 128, 128,
                            w1_ap, w1_n)
                kb.stream_w(kb.d[we3] if e == 16 else kb.d[we3][e], 0, D, mi * 128, 128,
                            w3_ap, w3_n)
                for kc in range(DCH):
                    P.op("tensor", (lambda kc=kc, n0=n0, w1_ap=w1_ap: lambda t: t.matmul(
                        kb.pb0.ap()[0:128, 0:512], w1_ap[:, kc * 128:(kc + 1) * 128],
                        xnbf.ap()[:, kc * T + n0:kc * T + n0 + 512],
                        start=(kc == 0), stop=(kc == DCH - 1)))(),
                        reads=[w1_n, xnbf.name], writes=["pb0"], tag="l2g_mm")
                for kc in range(DCH):
                    P.op("tensor", (lambda kc=kc, n0=n0, w3_ap=w3_ap: lambda t: t.matmul(
                        kb.pb1.ap()[0:128, 0:512], w3_ap[:, kc * 128:(kc + 1) * 128],
                        xnbf.ap()[:, kc * T + n0:kc * T + n0 + 512],
                        start=(kc == 0), stop=(kc == DCH - 1)))(),
                        reads=[w3_n, xnbf.name], writes=["pb1"], tag="l2u_mm")
                P.op("scalar", (lambda: lambda s: s.activation(
                    hs_ap, kb.pb0.ap()[0:128, 0:512], AF.Silu))(),
                    reads=["pb0"], writes=[hs_n], tag="l2_silu")
                P.op("vector", (lambda: lambda v: v.tensor_mul(
                    hs_ap, hs_ap, kb.pb1.ap()[0:128, 0:512]))(),
                    reads=[hs_n, "pb1"], writes=[hs_n], tag="l2_hu")
                if e < 16:
                    P.op("vector", (lambda mi=mi: lambda v: v.tensor_mul(
                        hb_ap[:, mi * 512:(mi + 1) * 512], hs_ap, combBC_ap))(),
                        reads=[hs_n, combBC_n], writes=[hb_n], tag="l2_hc")
                else:
                    P.op("vector", (lambda mi=mi: lambda v: v.tensor_copy(
                        hb_ap[:, mi * 512:(mi + 1) * 512], hs_ap))(),
                        reads=[hs_n], writes=[hb_n], tag="l2_hc")
            for oc in range(8):
                w2_ap, w2_n = kb.wslot(2)
                kb.stream_w(kb.d[we2] if e == 16 else kb.d[we2][e], 0, I_, oc * 128, 128,
                            w2_ap, w2_n)
                for ic in range(6):
                    P.op("tensor", (lambda ic=ic, oc=oc, w2_ap=w2_ap: lambda t: t.matmul(
                        kb.pbX.ap()[0:128, 0:512], w2_ap[:, ic * 128:(ic + 1) * 128],
                        hb_ap[:, ic * 512:(ic + 1) * 512],
                        start=(ic == 0), stop=(ic == 5)))(),
                        reads=[w2_n, hb_n], writes=["pbX"], tag="l2y_mm")
                if first:
                    P.op("vector", (lambda oc=oc: lambda v: v.tensor_copy(
                        yacc.ap()[:, oc * 512:(oc + 1) * 512], kb.pbX.ap()[0:128, 0:512]))(),
                        reads=["pbX"], writes=[("arf2", oc * 512, (oc + 1) * 512)], tag="l2y_ev")
                else:
                    P.op("vector", (lambda oc=oc: lambda v: v.tensor_add(
                        yacc.ap()[:, oc * 512:(oc + 1) * 512],
                        yacc.ap()[:, oc * 512:(oc + 1) * 512], kb.pbX.ap()[0:128, 0:512]))(),
                        reads=["pbX", ("arf2", oc * 512, (oc + 1) * 512)],
                        writes=[("arf2", oc * 512, (oc + 1) * 512)], tag="l2y_ev")
            first = False
        for oc in range(8):
            P.op("vector", (lambda oc=oc, n0=n0: lambda v: v.tensor_add(
                xres.ap()[:, oc * T + n0:oc * T + n0 + 512],
                xres.ap()[:, oc * T + n0:oc * T + n0 + 512],
                yacc.ap()[:, oc * 512:(oc + 1) * 512]))(),
                reads=[("arf2", oc * 512, (oc + 1) * 512), xres.name],
                writes=[xres.name], tag="l2xr_add")



LAYER_WEIGHTS = {
    1: ["L1_attn_in_proj_w", "L1_convW", "L1_convB", "L1_attn_dt_bias",
        "L1_attn_A_log", "L1_attn_Dr", "L1_attn_mnorm_w", "L1_attn_out_proj_w",
        "L1_attn_attn_w", "L1_attn_proj_w", "L1_maaKT", "L1_maaRT",
        "L1_attn_key_w", "L1_attn_recept_w", "L1_attn_value_w",
        "L1_ffn_Wqkv", "L1_qscol", "L1_kscol", "L1_ffn_Wo", "L1_lngT",
        "L1_lnbT", "L1_W1p", "L1_b1T", "L1_W2p", "L1_b2T",
        "C_l1cqF", "C_l1sqF", "C_l1ckF", "C_l1skF",
        "C_sel2", "C_sel2T", "C_l1rotP",
        "C_lm0", "C_lm1", "C_lm0w", "C_lm1w"],
    0: ["L0_convW", "L0_convBT", "L0_DT", "L0_dtprojBT", "L0_xprojT",
        "L0_dtprojT", "L0_BprojT", "L0_CprojT", "L0_routerT", "L0_attn_A_log",
        "L0_ffn_w1", "L0_ffn_w2", "L0_ffn_b1T", "L0_ffn_b2", "C_selall"],
    2: ["L2_attn_w_qkv", "L2_attn_w_out", "L2_attn_b_qkT", "L2_attn_b_v",
        "L2_attn_b_outT", "L2_ffn_gate_wT", "L2_ffn_gate_bias",
        "L2_ffn_w1T", "L2_ffn_w2T", "L2_ffn_w3T",
        "L2_ffn_ws1T", "L2_ffn_ws2T", "L2_ffn_ws3T",
        "C_l2rotP", "C_l2cos", "C_l2sin", "C_selall"],
    3: ["L3_attn_WqT", "L3_attn_WkT", "L3_attn_WvT", "L3_attn_WoT",
        "L3_ffn_w_gate", "L3_ffn_w_up", "L3_ffn_w_down"],
}

WEIGHT_SHAPES = {
    "L3_attn_WqT": [D, D], "L3_attn_WkT": [D, D], "L3_attn_WvT": [D, D],
    "L3_attn_WoT": [D, D],
    "L3_ffn_w_gate": [3072, D], "L3_ffn_w_up": [3072, D], "L3_ffn_w_down": [D, 3072],
    "L2_attn_w_qkv": [D, 3072], "L2_attn_w_out": [D, D],
    "L2_attn_b_qkT": [128, 16], "L2_attn_b_v": [1, 1024], "L2_attn_b_outT": [128, 8],
    "L2_ffn_gate_wT": [D, 16], "L2_ffn_gate_bias": [1, 16],
    "L2_ffn_w1T": [16, D, 768], "L2_ffn_w2T": [16, 768, D], "L2_ffn_w3T": [16, D, 768],
    "L2_ffn_ws1T": [D, 768], "L2_ffn_ws2T": [768, D], "L2_ffn_ws3T": [D, 768],
    "C_l2rotP": [128, 128], "C_l2cos": [128, T], "C_l2sin": [128, T],
    "C_selall": [16, 16 * 128],
    "L0_convW": [D, 4], "L0_convBT": [128, 8], "L0_DT": [128, 8],
    "L0_dtprojBT": [128, 8], "L0_xprojT": [D, 64], "L0_dtprojT": [64, D],
    "L0_BprojT": [D, 16], "L0_CprojT": [D, 16], "L0_routerT": [D, 8],
    "L0_attn_A_log": [D, 16],
    "L0_ffn_w1": [8, D, 4096], "L0_ffn_w2": [8, 4096, D],
    "L0_ffn_b1T": [8, 128, 32], "L0_ffn_b2": [8, D],
    "L1_attn_in_proj_w": [D, 4256], "L1_convW": [128, 68], "L1_convB": [128, 17],
    "L1_attn_dt_bias": [32, 1], "L1_attn_A_log": [32, 1], "L1_attn_Dr": [1, 32],
    "L1_attn_mnorm_w": [1, 2048], "L1_attn_out_proj_w": [2048, D],
    "L1_attn_attn_w": [D, 1152], "L1_attn_proj_w": [D, D],
    "L1_maaKT": [128, 8], "L1_maaRT": [128, 8],
    "L1_attn_key_w": [D, 4096], "L1_attn_recept_w": [D, D],
    "L1_attn_value_w": [4096, D],
    "L1_ffn_Wqkv": [6, D, 3072], "L1_qscol": [128, 6], "L1_kscol": [128, 6],
    "L1_ffn_Wo": [6, D, D], "L1_lngT": [6, 128, 8], "L1_lnbT": [6, 128, 8],
    "L1_W1p": [6, D, 5632], "L1_b1T": [6, 128, 44],
    "L1_W2p": [6, 2816, D], "L1_b2T": [6, 128, 8],
    "C_l1cqF": [128, T], "C_l1sqF": [128, T],
    "C_l1ckF": [128, T], "C_l1skF": [128, T],
    "C_sel2": [128, 2], "C_sel2T": [2, 128], "C_l1rotP": [128, 128],
    "C_lm0": [128, 512], "C_lm1": [128, 512],
    "C_lm0w": [128, 512], "C_lm1w": [128, 512],
}

WEIGHT_DTYPES = {"L1_W1p": "bf16", "L1_W2p": "bf16"}
for _nm in ['L0_ffn_w1', 'L0_ffn_w2', 'L2_ffn_w1T', 'L2_ffn_w2T', 'L2_ffn_w3T', 'L2_ffn_ws1T', 'L2_ffn_ws2T', 'L2_ffn_ws3T', 'L2_attn_w_qkv', 'L2_attn_w_out', 'L1_attn_in_proj_w', 'L1_attn_attn_w', 'L1_attn_out_proj_w', 'L1_attn_proj_w', 'L1_attn_key_w', 'L1_attn_recept_w', 'L1_attn_value_w', 'L1_ffn_Wqkv', 'L1_ffn_Wo', 'L3_attn_WqT', 'L3_attn_WkT', 'L3_attn_WvT', 'L3_attn_WoT']:
    WEIGHT_DTYPES[_nm] = "bf16"

CONST_SHAPES = {
    "C_ident": [128, 128], "C_identf": [128, 128], "C_ones": [1, 128],
    "C_onesb": [1, 128], "C_onescol": [128, 1], "C_iota": [128, 640],
    "C_cmask": [128, 128], "C_cmaskT": [128, 128],
}


import os
MK_PARTS = os.environ.get("MK_PARTS", "all")


def emit_layer(kb, layer, xres):
    P = kb.P
    xnbf = P.sb("xnbf", [128, DCH * T], BF16)
    nm_bc = P.sb("nm_bc", [128, T], F32)
    rs_bc = P.sb("rs_bc", [128, T], F32)
    kb.nm_bc_t = nm_bc
    kb.rs_bc_t = rs_bc
    emit_ln(kb, xres, xnbf, nm_bc, rs_bc, f"ln{layer}")
    do = lambda p: MK_PARTS == "all" or p in MK_PARTS.split(",")
    if layer == 3:
        if do("attn"):
            emit_l3_attn(kb, xres, xnbf)
        if do("ffn"):
            emit_l3_ffn(kb, xres, xnbf, "all")
    elif layer == 0:
        emit_l0(kb, xres, xnbf, nm_bc, rs_bc)
    elif layer == 1:
        emit_l1(kb, xres, xnbf, nm_bc, rs_bc)
    elif layer == 2:
        comb = kb.nm_bc_t
        comb_n = ("nm_bc", 0, 256)
        if do("ffn"):
            emit_l2_router(kb, xres, xnbf, nm_bc, rs_bc, comb, comb_n)
        if do("attn"):
            emit_l2_attn(kb, xres, xnbf)
        if do("ffn"):
            emit_l2_experts(kb, xres, xnbf, comb, comb_n)
    else:
        raise NotImplementedError(layer)


def build_layer_program(layers):
    from contextlib import ExitStack
    nc = bass.Bass("TRN2", target_bir_lowering=False)
    dram = {}
    xin = nc.dram_tensor("xin", [128, DCH * T], F32, kind="ExternalInput").ap()
    xout = nc.dram_tensor("xout", [128, DCH * T], F32, kind="ExternalOutput").ap()
    for nm, sh in CONST_SHAPES.items():
        dram[nm] = nc.dram_tensor(nm, sh, F32, kind="ExternalInput").ap()
    for layer in layers:
        for nm in LAYER_WEIGHTS[layer]:
            if nm in dram:
                continue
            wdt = BF16 if WEIGHT_DTYPES.get(nm) == "bf16" else F32
            dram[nm] = nc.dram_tensor(nm, WEIGHT_SHAPES[nm], wdt, kind="ExternalInput").ap()
    ctx = ExitStack()
    with ctx:
        P = Prog(nc, ctx)
        kb = Kb(P, dram)
        P.dma("gpsimd", kb.ident.ap(), dram["C_ident"][:, :], writes=["c_ident"])
        P.dma("gpsimd", kb.ones_rowb.ap(), dram["C_onesb"][:, :], writes=["c_onesb"])
        P.dma("gpsimd", kb.ones_colb.ap(), dram["C_onescol"][:, :], writes=["c_onescol"])
        for nm, sb in [("C_identf", kb.identf), ("C_ones", kb.ones_row)]:
            P.dma("sync", sb.ap(), dram[nm][:, :], writes=[sb.name])
        for nm, sb in [("C_cmask", kb.cmask),
                       ("C_cmaskT", kb.cmaskT)]:
            P.dma("gpsimd", sb.ap(), dram[nm][:, :], writes=[sb.name])
        xres = P.sb("xres", [128, DCH * T], F32)
        P.dma("sync", xres.ap(), xin[:, :], writes=["xres"])
        for li, layer in enumerate(layers):
            if li:
                P.layer_barrier()
            emit_layer(kb, layer, xres)
        P.dma("sync", xout[:, :], xres.ap(), reads=["xres"])
        P.finalize()
    return nc


def build_l3_stage(stage):
    from contextlib import ExitStack
    nc = bass.Bass("TRN2", target_bir_lowering=False)
    dram = {}
    for nm, sh in CONST_SHAPES.items():
        dram[nm] = nc.dram_tensor(nm, sh, F32, kind="ExternalInput").ap()
    for nm in LAYER_WEIGHTS[3]:
        dram[nm] = nc.dram_tensor(nm, WEIGHT_SHAPES[nm], F32, kind="ExternalInput").ap()
    xin = xout = s3xn = None
    if stage in (0, 4):
        xin = nc.dram_tensor("xin", [128, DCH * T], F32, kind="ExternalInput").ap()
        xout = nc.dram_tensor("xout", [128, DCH * T], F32, kind="ExternalOutput").ap()
    if stage == 0:
        s3xn = nc.dram_tensor("s3xn", [128, DCH * T], BF16, kind="ExternalOutput").ap()
    elif stage == 1:
        s3xn = nc.dram_tensor("s3xn", [128, DCH * T], BF16, kind="ExternalInput").ap()
    ctx = ExitStack()
    with ctx:
        P = Prog(nc, ctx)
        kb = Kb(P, dram)
        P.dma("gpsimd", kb.ident.ap(), dram["C_ident"][:, :], writes=["c_ident"])
        P.dma("gpsimd", kb.ones_rowb.ap(), dram["C_onesb"][:, :], writes=["c_onesb"])
        P.dma("gpsimd", kb.ones_colb.ap(), dram["C_onescol"][:, :], writes=["c_onescol"])
        for nm, sb in [("C_identf", kb.identf), ("C_ones", kb.ones_row)]:
            P.dma("sync", sb.ap(), dram[nm][:, :], writes=[sb.name])
        for nm, sb in [("C_cmask", kb.cmask), ("C_cmaskT", kb.cmaskT)]:
            P.dma("gpsimd", sb.ap(), dram[nm][:, :], writes=[sb.name])
        xnbf = P.sb("xnbf", [128, DCH * T], BF16)
        xres = None
        if stage in (0, 4):
            xres = P.sb("xres", [128, DCH * T], F32)
            P.dma("sync", xres.ap(), xin[:, :], writes=["xres"])
        if stage == 0:
            nm_bc = P.sb("nm_bc", [128, T], F32)
            rs_bc = P.sb("rs_bc", [128, T], F32)
            kb.nm_bc_t = nm_bc
            kb.rs_bc_t = rs_bc
            emit_ln(kb, xres, xnbf, nm_bc, rs_bc, "ln3")
            P.dma("sync", s3xn[:, :], xnbf.ap(), reads=["xnbf"])
            emit_l3_ffn(kb, xres, xnbf, stage=0)
            emit_l3_attn(kb, xres, xnbf)
            P.dma("sync", xout[:, :], xres.ap(), reads=["xres"])
        else:
            if stage == 1:
                P.dma("sync", xnbf.ap(), s3xn[:, :], writes=["xnbf"])
            emit_l3_ffn(kb, xres, xnbf, stage=stage)
            if stage == 4:
                P.dma("sync", xout[:, :], xres.ap(), reads=["xres"])
        P.finalize()
    return nc



def host_weight_maps(inp):
    cc = np.ascontiguousarray
    w = {}
    w["L3_attn_WqT"] = cc(inp["L3_attn_Wq"].T)
    w["L3_attn_WkT"] = cc(inp["L3_attn_Wk"].T)
    w["L3_attn_WvT"] = cc(inp["L3_attn_Wv"].T)
    w["L3_attn_WoT"] = cc(inp["L3_attn_Wo"].T)
    for nm in ["L3_ffn_w_gate", "L3_ffn_w_up", "L3_ffn_w_down"]:
        w[nm] = cc(inp[nm])
    w["L2_attn_w_qkv"] = cc(inp["L2_attn_w_qkv"])
    w["L2_attn_w_out"] = cc(inp["L2_attn_w_out"])
    b = inp["L2_attn_b_qkv"]
    w["L2_attn_b_qkT"] = cc(np.concatenate(
        [b[0:1024].reshape(8, 128).T, b[1024:2048].reshape(8, 128).T], axis=1))
    w["L2_attn_b_v"] = cc(b[2048:3072][None, :])
    w["L2_attn_b_outT"] = cc(inp["L2_attn_b_out"].reshape(8, 128).T)
    w["L2_ffn_gate_wT"] = cc(inp["L2_ffn_gate_w"].T)
    w["L2_ffn_gate_bias"] = cc(inp["L2_ffn_gate_bias"][None, :])
    w["L2_ffn_w1T"] = cc(inp["L2_ffn_w1"].transpose(0, 2, 1))
    w["L2_ffn_w2T"] = cc(inp["L2_ffn_w2"].transpose(0, 2, 1))
    w["L2_ffn_w3T"] = cc(inp["L2_ffn_w3"].transpose(0, 2, 1))
    w["L2_ffn_ws1T"] = cc(inp["L2_ffn_ws1"].T)
    w["L2_ffn_ws2T"] = cc(inp["L2_ffn_ws2"].T)
    w["L2_ffn_ws3T"] = cc(inp["L2_ffn_ws3"].T)
    Pm = np.zeros((128, 128), np.float32)
    for h in (0, 64):
        for i in range(32):
            Pm[h + i, h + i + 32] = -1.0
            Pm[h + i + 32, h + i] = 1.0
    w["C_l2rotP"] = cc(Pm.T)
    half = 32
    freqs = 1.0 / (500.0 ** (np.arange(half, dtype=np.float32) / half))
    ang = np.arange(T, dtype=np.float32)[:, None] * freqs[None, :]
    cosr = np.cos(ang).T
    sinr = np.sin(ang).T
    ctab = np.zeros((128, T), np.float32)
    stab = np.zeros((128, T), np.float32)
    for h in (0, 64):
        ctab[h:h + 32] = cosr
        ctab[h + 32:h + 64] = cosr
        stab[h:h + 32] = sinr
        stab[h + 32:h + 64] = sinr
    w["C_l2cos"] = cc(ctab)
    w["C_l2sin"] = cc(stab)
    sel = np.zeros((16, 16 * 128), np.float32)
    for e in range(16):
        sel[e, e * 128:(e + 1) * 128] = 1.0
    w["C_selall"] = cc(sel)
    w["L0_convW"] = cc(inp["L0_attn_conv_w"].reshape(D, 4))
    w["L0_convBT"] = cc(inp["L0_attn_conv_b"].reshape(8, 128).T)
    w["L0_DT"] = cc(inp["L0_attn_D_param"].reshape(8, 128).T)
    w["L0_dtprojBT"] = cc(inp["L0_attn_dt_proj_b"].reshape(8, 128).T)
    w["L0_xprojT"] = cc(inp["L0_attn_x_proj_dt_w"].T)
    w["L0_dtprojT"] = cc(inp["L0_attn_dt_proj_w"].T)
    w["L0_BprojT"] = cc(inp["L0_attn_B_proj_w"].T)
    w["L0_CprojT"] = cc(inp["L0_attn_C_proj_w"].T)
    w["L0_routerT"] = cc(inp["L0_ffn_router_w"].T)
    w["L0_attn_A_log"] = cc(inp["L0_attn_A_log"])
    w["L0_ffn_w1"] = cc(inp["L0_ffn_w1"])
    w["L0_ffn_w2"] = cc(inp["L0_ffn_w2"])
    w["L0_ffn_b1T"] = cc(inp["L0_ffn_b1"].reshape(8, 32, 128).transpose(0, 2, 1))
    w["L0_ffn_b2"] = cc(inp["L0_ffn_b2"])
    w["L1_attn_in_proj_w"] = cc(inp["L1_attn_in_proj_w"])
    w["L1_convW"] = cc(inp["L1_attn_conv_w"].reshape(17, 128, 4)
                       .transpose(1, 0, 2).reshape(128, 68))
    w["L1_convB"] = cc(inp["L1_attn_conv_b"].reshape(17, 128).T)
    w["L1_attn_dt_bias"] = cc(inp["L1_attn_dt_bias"][:, None])
    w["L1_attn_A_log"] = cc(inp["L1_attn_A_log"][:, None])
    w["L1_attn_Dr"] = cc(inp["L1_attn_D"][None, :])
    w["L1_attn_mnorm_w"] = cc(inp["L1_attn_mnorm_w"][None, :])
    w["L1_attn_out_proj_w"] = cc(inp["L1_attn_out_proj_w"])
    w["L1_attn_attn_w"] = cc(inp["L1_attn_attn_w"])
    w["L1_attn_proj_w"] = cc(inp["L1_attn_proj_w"])
    w["L1_maaKT"] = cc(inp["L1_attn_time_maa_k"].reshape(8, 128).T)
    w["L1_maaRT"] = cc(inp["L1_attn_time_maa_r"].reshape(8, 128).T)
    w["L1_attn_key_w"] = cc(inp["L1_attn_key_w"])
    w["L1_attn_recept_w"] = cc(inp["L1_attn_recept_w"])
    w["L1_attn_value_w"] = cc(inp["L1_attn_value_w"])
    w["L1_ffn_Wqkv"] = cc(inp["L1_ffn_Wqkv"])
    w["L1_qscol"] = cc(np.tile(inp["L1_ffn_q_scale"], (1, 2)).T)
    w["L1_kscol"] = cc(np.tile(inp["L1_ffn_k_scale"], (1, 2)).T)
    w["L1_ffn_Wo"] = cc(inp["L1_ffn_Wo"])
    w["L1_lngT"] = cc(inp["L1_ffn_ln_g"].reshape(6, 8, 128).transpose(0, 2, 1))
    w["L1_lnbT"] = cc(inp["L1_ffn_ln_b"].reshape(6, 8, 128).transpose(0, 2, 1))
    import ml_dtypes
    w1p = np.zeros((6, D, 5632), np.float32)
    w1p[:, :, 0:2730] = inp["L1_ffn_W1"][:, :, 0:2730]
    w1p[:, :, 2816:5546] = inp["L1_ffn_W1"][:, :, 2730:5460]
    w["L1_W1p"] = np.ascontiguousarray(w1p.astype(ml_dtypes.bfloat16))
    b1p = np.zeros((6, 5632), np.float32)
    b1p[:, 0:2730] = inp["L1_ffn_b1"][:, 0:2730]
    b1p[:, 2816:5546] = inp["L1_ffn_b1"][:, 2730:5460]
    w["L1_b1T"] = cc(b1p.reshape(6, 44, 128).transpose(0, 2, 1))
    w2p = np.zeros((6, 2816, D), np.float32)
    w2p[:, 0:2730, :] = inp["L1_ffn_W2"]
    w["L1_W2p"] = np.ascontiguousarray(w2p.astype(ml_dtypes.bfloat16))
    w["L1_b2T"] = cc(inp["L1_ffn_b2"].reshape(6, 8, 128).transpose(0, 2, 1))
    m64 = np.zeros((64, 64), np.float32)
    m64[np.arange(32) + 32, np.arange(32)] = -1.0
    m64[np.arange(32), np.arange(32) + 32] = 1.0
    rp = np.zeros((128, 128), np.float32)
    rp[0:64, 0:64] = m64
    rp[64:128, 64:128] = m64
    w["C_l1rotP"] = cc(rp)
    s2 = np.zeros((128, 2), np.float32)
    s2[0:64, 0] = 1.0
    s2[64:128, 1] = 1.0
    w["C_sel2"] = cc(s2)
    w["C_sel2T"] = cc(s2.T)
    half = np.arange(0, 64, 2, dtype=np.float32)
    inv_freq = 1.0 / 10000.0 ** (half / 64.0)
    tt = np.arange(T, dtype=np.float32)
    fr = tt[:, None] * inv_freq[None, :]
    fr = np.concatenate([fr, fr], -1)
    base_scale = (half + 0.4 * 64.0) / (1.4 * 64.0)
    power = (tt - T // 2) / 128.0
    scl = base_scale[None, :] ** power[:, None]
    scl = np.concatenate([scl, scl], -1)
    cq = np.cos(fr) * scl
    sq = np.sin(fr) * scl
    ck = np.cos(fr) / scl
    sk = np.sin(fr) / scl
    def _fm(tab):
        return cc(np.vstack([tab.T, tab.T]))
    w["C_l1cqF"] = _fm(cq)
    w["C_l1sqF"] = _fm(sq)
    w["C_l1ckF"] = _fm(ck)
    w["C_l1skF"] = _fm(sk)
    for s in (0, 1):
        i = np.arange(128)[:, None] + 128 * s
        j = np.arange(512)[None, :]
        ok = (j - 256 <= i) & (j - 256 >= i - 256)
        m = np.where(ok, 0.0, -1e30).astype(np.float32)
        w[f"C_lm{s}"] = cc(m)
        okw = ok & (j >= 256)
        w[f"C_lm{s}w"] = cc(np.where(okw, 0.0, -1e30).astype(np.float32))
    for nm, dt_ in WEIGHT_DTYPES.items():
        if dt_ == "bf16" and nm in w and w[nm].dtype != ml_dtypes.bfloat16:
            w[nm] = np.ascontiguousarray(w[nm].astype(ml_dtypes.bfloat16))
    return w


def to_fm(xb):
    return np.ascontiguousarray(
        xb.T.reshape(DCH, 128, T).transpose(1, 0, 2).reshape(128, DCH * T))


def from_fm(xf):
    return np.ascontiguousarray(
        xf.reshape(128, DCH, T).transpose(1, 0, 2).reshape(D, T).T)


def _declared_inputs(nc):
    import concourse.mybir as _mb
    declared = set()
    for a in nc.m.functions[0].allocations:
        if isinstance(a, _mb.MemoryLocationSet) and a.kind == "ExternalInput":
            declared.add(a.memorylocations[0].name)
    return declared


def _launch(nc, allm, per_core):
    declared = _declared_inputs(nc)
    in_maps = []
    for b in range(N_CORES):
        m = {k: v for k, v in allm.items() if k in declared}
        m.update({k: v for k, v in per_core[b].items() if k in declared})
        in_maps.append(m)
    res = run_bass_kernel_spmd(nc, in_maps, core_ids=list(range(N_CORES)))
    return res.results


def kernel(**inp):
    consts = Kb.host_consts()
    wmaps = host_weight_maps(inp)
    x = inp["L0_attn_x"]
    allm = {}
    allm.update(consts)
    allm.update(wmaps)
    env_layers = os.environ.get("MK_LAYERS", "")
    if env_layers:
        nc = build_layer_program([int(c) for c in env_layers])
        results = _launch(nc, allm, [{"xin": to_fm(x[b])} for b in range(N_CORES)])
        return np.stack([from_fm(r["xout"]) for r in results])
    if not os.environ.get("MK_MULTI"):
        import time as _t
        import sys as _sys
        t0 = _t.time()
        nc = build_layer_program([0, 1, 2, 3])
        t1 = _t.time()
        print(f"[mk] build {t1 - t0:.1f}s", file=_sys.stderr, flush=True)
        results = _launch(nc, allm, [{"xin": to_fm(x[b])} for b in range(N_CORES)])
        print(f"[mk] launch {_t.time() - t1:.1f}s", file=_sys.stderr, flush=True)
        return np.stack([from_fm(r["xout"]) for r in results])
    cur = [to_fm(x[b]) for b in range(N_CORES)]
    for L in (0, 1, 2):
        nc = build_layer_program([L])
        results = _launch(nc, allm, [{"xin": cur[b]} for b in range(N_CORES)])
        cur = [r["xout"] for r in results]
    gm = np.zeros((1, 40), np.float32)
    nc = build_l3_stage(0)
    r0 = _launch(nc, allm, [{"xin": cur[b]} for b in range(N_CORES)])
    xA = [r["xout"] for r in r0]
    s3 = [r["s3xn"] for r in r0]
    for b in range(N_CORES):
        gm[0, b] = r0[b]["pmax"][0, 0]
    nc = build_l3_stage(1)
    r1 = _launch(nc, allm, [{"s3xn": s3[b], "gmax": gm} for b in range(N_CORES)])
    for b in range(N_CORES):
        gm[0, 8 + b] = r1[b]["pmax"][0, 0]
        gm[0, 16 + b] = r1[b]["pmax"][0, 1]
    nc = build_l3_stage(2)
    r2 = _launch(nc, allm, [{"sp_up": r1[b]["sp_up"], "sp_gate": r1[b]["sp_gate"],
                             "gmax": gm} for b in range(N_CORES)])
    for b in range(N_CORES):
        gm[0, 24 + b] = r2[b]["pmax"][0, 0]
    nc = build_l3_stage(3)
    r3 = _launch(nc, allm, [{"sp_up": r1[b]["sp_up"], "sp_p3": r2[b]["sp_p3"],
                             "gmax": gm} for b in range(N_CORES)])
    for b in range(N_CORES):
        gm[0, 32 + b] = r3[b]["pmax"][0, 0]
    nc = build_l3_stage(4)
    r4 = _launch(nc, allm, [{"xin": xA[b], "sp_opre": r3[b]["sp_opre"],
                             "gmax": gm} for b in range(N_CORES)])
    return np.stack([from_fm(r["xout"]) for r in r4])
